# version 04
# baseline (speedup 1.0000x reference)
import numpy as np
import concourse.bass as bass
import concourse.mybir as mybir
from concourse.bass_utils import run_bass_kernel_spmd

F32 = mybir.dt.float32
BF16 = mybir.dt.bfloat16
FP8 = mybir.dt.float8e4
ALU = mybir.AluOpType
AF = mybir.ActivationFunctionType
AX = mybir.AxisListType

T_LOC = 1024
C = 2048
H = 16
S = 128
I = 8192
NT = T_LOC // 128
NCT = C // 128
NIT = I // 128

R448 = float(np.float32(1.0) / np.float32(448.0))
R448_2 = float(np.float32(2.0) * (np.float32(1.0) / np.float32(448.0)))


def build_nc(debug=()):
    nc = bass.Bass("TRN2", target_bir_lowering=False, debug=False,
                   detect_race_conditions=False)
    x_tm = nc.dram_tensor("x_tm", [T_LOC, C], F32, kind="ExternalInput").ap()
    WkT = nc.dram_tensor("WkT", [C, C], F32, kind="ExternalInput").ap()
    WvT = nc.dram_tensor("WvT", [C, C], F32, kind="ExternalInput").ap()
    WrT = nc.dram_tensor("WrT", [C, C], F32, kind="ExternalInput").ap()
    WoT = nc.dram_tensor("WoT", [C, C], F32, kind="ExternalInput").ap()
    gw = nc.dram_tensor("gw", [I, C], F32, kind="ExternalInput").ap()
    uw = nc.dram_tensor("uw", [I, C], F32, kind="ExternalInput").ap()
    dw = nc.dram_tensor("dw", [C, I], F32, kind="ExternalInput").ap()
    tdT = nc.dram_tensor("tdT", [S, H], F32, kind="ExternalInput").ap()
    tfT = nc.dram_tensor("tfT", [S, H], F32, kind="ExternalInput").ap()
    cwi = nc.dram_tensor("cwi", [S, 8], F32, kind="ExternalInput").ap()
    c0f = nc.dram_tensor("c0f", [S, 1], F32, kind="ExternalInput").ap()
    out = nc.dram_tensor("out", [T_LOC, C], F32, kind="ExternalOutput").ap()
    sp_xn = nc.dram_tensor("sp_xn", [T_LOC, C], BF16).ap()
    sp_xq = nc.dram_tensor("sp_xq", [T_LOC, C], BF16).ap()
    sp_gw = nc.dram_tensor("sp_gw", [I, C], BF16).ap()
    sp_uw = nc.dram_tensor("sp_uw", [I, C], BF16).ap()
    sp_dw = nc.dram_tensor("sp_dw", [C, I], BF16).ap()
    sp_hq = nc.dram_tensor("sp_hq", [T_LOC, I], BF16).ap()
    sp_rw = nc.dram_tensor("sp_rw", [C, T_LOC], BF16).ap()
    cb_in = nc.dram_tensor("cb_in", [S, 2 * H], F32)
    cb_out = nc.dram_tensor("cb_out", [8, S, 2 * H], F32)

    dbg = {}
    for name, shape, dt in (
        ("rwkvT", [C, T_LOC], BF16), ("gwq", [I, C], BF16),
        ("dwq", [C, I], BF16), ("hq", [T_LOC, I], BF16),
    ):
        if name in debug:
            dbg[name] = nc.dram_tensor("dbg_" + name, shape, dt,
                                       kind="ExternalOutput").ap()

    live = {}

    def sb(name, shape, dt):
        cm = nc.sbuf_tensor(name, shape, dt)
        t = cm.__enter__()
        live[name] = cm
        return t

    def free(*names):
        for n in names:
            live.pop(n).__exit__(None, None, None)

    ctx = []

    def ps_(name):
        cm = nc.psum_tensor(name, [128, 512], F32)
        t = cm.__enter__()
        ctx.append(cm)
        return t

    def semi(name):
        cm = nc.semaphore(name)
        s = cm.__enter__()
        ctx.append(cm)
        return s

    stat = sb("stat", [128, 16], F32)
    bm64 = sb("bm64", [128, 64], F32)
    s2x = sb("s2x", [128, 64], F32)
    rs2 = sb("rs2", [128, 64], F32)
    ceps5 = sb("ceps5", [128, 1], F32)
    ceps6 = sb("ceps6", [128, 1], F32)
    zero1 = sb("zero1", [128, 1], F32)
    tds = sb("tds", [128, H], F32)
    tfs = sb("tfs", [128, H], F32)
    cws = sb("cws", [128, 8], F32)
    c0s = sb("c0s", [128, 1], F32)
    e_all = sb("e_all", [128, H], F32)
    d_all = sb("d_all", [128, H], F32)
    fe_all = sb("fe_all", [128, H], F32)
    Lo_n = sb("Lo_n", [128, H], F32)
    Lo_d = sb("Lo_d", [128, H], F32)
    Lg = sb("Lg", [128, 8 * 2 * H], F32)
    car_n = sb("car_n", [128, H], F32)
    car_d = sb("car_d", [128, H], F32)
    co16 = sb("co16", [128, H], F32)
    tt16 = sb("tt16", [128, H], F32)

    ps = [ps_("psb%d" % i) for i in range(8)]

    s_ld = semi("s_ld")
    s_gp = semi("s_gp")
    s_tr = semi("s_tr")
    s_sp = semi("s_sp")
    s_pe = semi("s_pe")
    s_dv = semi("s_dv")
    s_ac = semi("s_ac")
    s_gc = semi("s_gc")
    s_cc = semi("s_cc")
    s_out = semi("s_out")

    class V:
        ld = gp = tr = sp = pe = dv = ac = gc = cc = outv = 0

    with nc.Block() as blk:

        def barrier():
            pairs = [(s_ld, V.ld), (s_gp, V.gp), (s_tr, V.tr), (s_sp, V.sp),
                     (s_pe, V.pe), (s_dv, V.dv), (s_ac, V.ac), (s_gc, V.gc)]
            for deco in (blk.sync, blk.scalar, blk.vector, blk.gpsimd, blk.tensor):
                @deco
                def _(eng, pairs=pairs):
                    for sem, val in pairs:
                        if val:
                            eng.wait_ge(sem, val)

        scn = sb("scn", [128, T_LOC], F32)
        scd = sb("scd", [128, T_LOC], F32)
        numl = sb("numl", [128, H * T_LOC], BF16)
        denl = sb("denl", [128, H * T_LOC], BF16)
        sigr = sb("sigr", [128, H * T_LOC], BF16)

        xbuf = sb("xbuf", [128, NT * C], F32)
        xn32 = sb("xn32", [128, C], F32)
        xc32 = sb("xc32", [128, C], F32)
        sq32 = sb("sq32", [128, C], F32)
        xnbf = sb("xnbf", [128, C], BF16)
        xqbf = sb("xqbf", [128, C], BF16)
        q8a = sb("q8a", [128, C], FP8)

        @blk.scalar
        def _(eng):
            eng.dma_start(out=xbuf[:, :].rearrange("p (t c) -> p t c", t=NT),
                          in_=x_tm.rearrange("(t p) c -> p t c", p=128)).then_inc(s_ld, 16)
            V.ld += 16
        ld_x = V.ld

        @blk.gpsimd
        def _(eng):
            eng.dma_start(out=tds[:, :], in_=tdT).then_inc(s_gp, 16)
            eng.dma_start(out=tfs[:, :], in_=tfT).then_inc(s_gp, 16)
            eng.dma_start(out=cws[:, :], in_=cwi).then_inc(s_gp, 16)
            eng.dma_start(out=c0s[:, :], in_=c0f).then_inc(s_gp, 16)
            V.gp += 64
        gp_const = V.gp

        @blk.vector
        def _(eng):
            eng.memset(ceps5[:, :], 1e-5)
            eng.memset(ceps6[:, :], 1e-6)
            eng.memset(zero1[:, :], 0.0)
            eng.nop().then_inc(s_dv, 1)
            V.dv += 1

        for tt in range(NT):
            xt = xbuf[:, tt * C:(tt + 1) * C]

            @blk.vector
            def _(eng, xt=xt):
                eng.wait_ge(s_ld, ld_x)
                eng.wait_ge(s_ac, V.ac)
                eng.wait_ge(s_sp, V.sp)
                eng.tensor_reduce(out=stat[:, 0:1], in_=xt, axis=AX.X, op=ALU.add,
                                  negate=True)
                eng.drain()
                eng.tensor_scalar(out=stat[:, 1:2], in0=stat[:, 0:1],
                                  scalar1=float(1.0 / C), scalar2=None, op0=ALU.mult)
                eng.drain()
                eng.tensor_scalar(out=xc32[:, :], in0=xt, scalar1=stat[:, 1:2],
                                  scalar2=None, op0=ALU.add)
                eng.drain()
                eng.scalar_tensor_tensor(out=sq32[:, :], in0=xc32[:, :], scalar=1.0,
                                         in1=xc32[:, :], op0=ALU.mult, op1=ALU.mult,
                                         accum_out=stat[:, 2:3]).then_inc(s_dv, 1)
                V.dv += 1
            dv_xc = V.dv

            @blk.scalar
            def _(eng, dv_xc=dv_xc):
                eng.wait_ge(s_dv, dv_xc)
                eng.activation(out=stat[:, 3:4], in_=stat[:, 2:3], func=AF.Sqrt,
                               scale=float(1.0 / C), bias=ceps5[:, 0:1]).then_inc(s_ac, 1)
                V.ac += 1
            ac_std = V.ac

            @blk.vector
            def _(eng, ac_std=ac_std):
                eng.wait_ge(s_ac, ac_std)
                eng.reciprocal(out=stat[:, 4:5], in_=stat[:, 3:4])
                eng.drain()
                eng.tensor_scalar(out=xn32[:, :], in0=xc32[:, :], scalar1=stat[:, 4:5],
                                  scalar2=None, op0=ALU.mult)
                eng.drain()
                eng.tensor_copy(out=xnbf[:, :], in_=xn32[:, :])
                eng.tensor_reduce(out=bm64[:, 0:64],
                                  in_=xn32[:, :].rearrange("p (b k) -> p b k", k=32),
                                  axis=AX.X, op=ALU.max, apply_absolute_value=True)
                eng.drain()
                eng.tensor_scalar(out=s2x[:, 0:64], in0=bm64[:, 0:64], scalar1=R448_2,
                                  scalar2=2e-30, op0=ALU.mult, op1=ALU.max)
                eng.drain()
                eng.reciprocal(out=rs2[:, 0:64], in_=s2x[:, 0:64])
                eng.drain()
                eng.tensor_tensor(out=q8a[:, :].rearrange("p (b k) -> p b k", k=32),
                                  in0=xn32[:, :].rearrange("p (b k) -> p b k", k=32),
                                  in1=rs2[:, 0:64].rearrange("p (b o) -> p b o", o=1
                                      ).broadcast_to([128, 64, 32]),
                                  op=ALU.mult)
                eng.drain()
                eng.tensor_tensor(out=xqbf[:, :].rearrange("p (b k) -> p b k", k=32),
                                  in0=q8a[:, :].rearrange("p (b k) -> p b k", k=32),
                                  in1=s2x[:, 0:64].rearrange("p (b o) -> p b o", o=1
                                      ).broadcast_to([128, 64, 32]),
                                  op=ALU.mult).then_inc(s_dv, 1)
                V.dv += 1
            dv_xq = V.dv

            @blk.scalar
            def _(eng, tt=tt, dv_xq=dv_xq):
                eng.wait_ge(s_dv, dv_xq)
                eng.dma_start(out=sp_xn[tt * 128:(tt + 1) * 128, :],
                              in_=xnbf[:, :]).then_inc(s_sp, 16)
                eng.dma_start(out=sp_xq[tt * 128:(tt + 1) * 128, :],
                              in_=xqbf[:, :]).then_inc(s_sp, 16)
                V.sp += 32
        sp_xnxq = V.sp
        free("q8a", "xqbf", "xnbf", "sq32", "xc32", "xn32", "xbuf")

        xnT = sb("xnT", [128, NCT * T_LOC], BF16)
        wbufs = [(sb("wkt0", [128, NCT * 128], BF16),
                  sb("wvt0", [128, NCT * 128], BF16),
                  sb("wrt0", [128, NCT * 128], BF16))] * 2
        w_h = sb("w_h", [128, T_LOC], F32)
        a_h = sb("a_h", [128, T_LOC], F32)
        natF = [sb("natF%d" % i, [128, C], F32) for i in range(4)]
        qoutF = [sb("qoutF%d" % i, [128, C], BF16) for i in range(2)]
        q8F = [sb("q8F%d" % i, [128, C], FP8) for i in range(2)]
        bmF = [sb("bmF%d" % i, [128, 64], F32) for i in range(4)]
        s2F = [sb("s2F%d" % i, [128, 64], F32) for i in range(4)]
        rsF = [sb("rsF%d" % i, [128, 64], F32) for i in range(4)]

        @blk.sync
        def _(eng):
            eng.wait_ge(s_sp, sp_xnxq)
            for ct in range(NCT):
                eng.dma_start_transpose(
                    out=xnT[:, ct * T_LOC:(ct + 1) * T_LOC],
                    in_=sp_xn[:, ct * 128:(ct + 1) * 128]).then_inc(s_tr, 16)
                V.tr += 16
        tr_xn = V.tr

        @blk.scalar
        def _(eng):
            eng.wait_ge(s_gp, gp_const)
            eng.activation(out=e_all[:, :], in_=tds[:, :], func=AF.Exp)
            eng.drain()
            eng.activation(out=d_all[:, :], in_=e_all[:, :], func=AF.Exp, scale=-1.0)
            eng.activation(out=fe_all[:, :], in_=tfs[:, :], func=AF.Exp).then_inc(s_ac, 1)
            V.ac += 1
        ac_decay = V.ac

        @blk.vector
        def _(eng):
            eng.wait_ge(s_ac, ac_decay)
            eng.scalar_tensor_tensor(out=fe_all[:, :], in0=fe_all[:, :], scalar=-1.0,
                                     in1=c0s[:, 0:1].broadcast_to([128, H]),
                                     op0=ALU.add, op1=ALU.mult)
            eng.drain()
            eng.tensor_scalar(out=fe_all[:, :], in0=fe_all[:, :], scalar1=1.0,
                              scalar2=None, op0=ALU.add).then_inc(s_dv, 1)
            V.dv += 1
        dv_fe = V.dv

        funits = []
        for it in range(NIT):
            funits.append((gw, sp_gw, it, 0))
        for it in range(NIT):
            funits.append((uw, sp_uw, it, 0))
        for ct in range(NCT):
            for cs in range(4):
                funits.append((dw, sp_dw, ct, cs * C))

        gc_wq = []
        dv_sc_l = []
        dv_wq = []
        sp_wq = []

        def emit_funit_front(g):
            src, dst, rt, co = funits[g]
            p = g % 4
            nat = natF[p]
            bmp, s2p, rsp = bmF[p], s2F[p], rsF[p]

            @blk.scalar
            def _(eng, g=g, nat=nat, src=src, rt=rt, co=co):
                if g >= 4:
                    eng.wait_ge(s_gc, gc_wq[g - 4])
                eng.dma_start(out=nat[:, :],
                              in_=src[rt * 128:(rt + 1) * 128, co:co + C]).then_inc(s_ld, 16)
                V.ld += 16
            ld_nat = V.ld

            @blk.vector
            def _(eng, g=g, ld_nat=ld_nat, nat=nat, bmp=bmp, s2p=s2p, rsp=rsp):
                eng.wait_ge(s_ld, ld_nat)
                if g >= 4:
                    eng.wait_ge(s_gc, gc_wq[g - 4])
                eng.tensor_reduce(out=bmp[:, :],
                                  in_=nat[:, :].rearrange("p (b k) -> p b k", k=32),
                                  axis=AX.X, op=ALU.max, apply_absolute_value=True)
                eng.drain()
                eng.tensor_scalar(out=s2p[:, :], in0=bmp[:, :], scalar1=R448_2,
                                  scalar2=2e-30, op0=ALU.mult, op1=ALU.max)
                eng.drain()
                eng.reciprocal(out=rsp[:, :], in_=s2p[:, :]).then_inc(s_dv, 1)
                V.dv += 1
            dv_sc_l.append(V.dv)

            @blk.gpsimd
            def _(eng, g=g, nat=nat, rsp=rsp):
                q8w = q8F[g % 2]
                eng.wait_ge(s_dv, dv_sc_l[g])
                if g >= 2:
                    eng.wait_ge(s_dv, dv_wq[g - 2])
                eng.tensor_tensor(out=q8w[:, :].rearrange("p (b k) -> p b k", k=32),
                                  in0=nat[:, :].rearrange("p (b k) -> p b k", k=32),
                                  in1=rsp[:, :].rearrange("p (b o) -> p b o", o=1
                                      ).broadcast_to([128, 64, 32]),
                                  op=ALU.mult).then_inc(s_gc, 1)
                V.gc += 1
            gc_wq.append(V.gc)

        def emit_funit_back(g):
            src, dst, rt, co = funits[g]
            qout, q8w, s2p = qoutF[g % 2], q8F[g % 2], s2F[g % 4]

            @blk.vector
            def _(eng, g=g, qout=qout, q8w=q8w, s2p=s2p):
                eng.wait_ge(s_gc, gc_wq[g])
                if g >= 2:
                    eng.wait_ge(s_sp, sp_wq[g - 2])
                eng.tensor_tensor(out=qout[:, :].rearrange("p (b k) -> p b k", k=32),
                                  in0=q8w[:, :].rearrange("p (b k) -> p b k", k=32),
                                  in1=s2p[:, :].rearrange("p (b o) -> p b o", o=1
                                      ).broadcast_to([128, 64, 32]),
                                  op=ALU.mult).then_inc(s_dv, 1)
                V.dv += 1
            dv_wq.append(V.dv)

            @blk.scalar
            def _(eng, g=g, qout=qout, dst=dst, rt=rt, co=co):
                eng.wait_ge(s_dv, dv_wq[g])
                eng.dma_start(out=dst[rt * 128:(rt + 1) * 128, co:co + C],
                              in_=qout[:, :]).then_inc(s_sp, 16)
                V.sp += 16
            sp_wq.append(V.sp)

        gp_head = []
        pe_headmm = []
        dv_headev = []
        ac_headsig = []

        def emit_head(h):
            wk_b, wv_b, wr_b = wbufs[h % 2]

            @blk.gpsimd
            def _(eng, h=h, wk_b=wk_b, wv_b=wv_b, wr_b=wr_b):
                if h == 0:
                    eng.wait_ge(s_sp, sp_xnxq)
                if h >= 1:
                    eng.wait_ge(s_pe, pe_headmm[h - 1])
                for W, buf in ((WkT, wk_b), (WvT, wv_b), (WrT, wr_b)):
                    eng.dma_start(
                        out=buf[:, :].rearrange("p (c n) -> p c n", c=NCT),
                        in_=W.rearrange("(c p) n -> p c n", p=128)[:, :, h * 128:(h + 1) * 128],
                    ).then_inc(s_gp, 16)
                    V.gp += 16
            gp_head.append(V.gp)

            @blk.tensor
            def _(eng, h=h, wk_b=wk_b, wv_b=wv_b, wr_b=wr_b):
                eng.wait_ge(s_gp, gp_head[h])
                eng.wait_ge(s_tr, tr_xn)
                if h >= 1:
                    eng.wait_ge(s_dv, dv_headev[h - 1])
                    eng.wait_ge(s_ac, ac_headsig[h - 1])
                mm = None
                for mi, buf in ((0, wk_b), (1, wv_b), (2, wr_b)):
                    for half in range(2):
                        bank = ps[mi * 2 + half]
                        for ci in range(NCT):
                            mm = eng.matmul(bank[:, :],
                                            buf[:, ci * 128:(ci + 1) * 128],
                                            xnT[:, ci * T_LOC + half * 512: ci * T_LOC + half * 512 + 512],
                                            start=(ci == 0), stop=(ci == NCT - 1))
                mm.then_inc(s_pe, 1)
                V.pe += 1
            pe_headmm.append(V.pe)

            @blk.vector
            def _(eng, h=h):
                eng.wait_ge(s_pe, pe_headmm[h])
                eng.wait_ge(s_dv, dv_fe)
                eng.wait_ge(s_ac, V.ac)
                eng.tensor_scalar(out=w_h[:, 0:512], in0=ps[0][:, :], scalar1=-10.0,
                                  scalar2=10.0, op0=ALU.max, op1=ALU.min)
                eng.tensor_scalar(out=w_h[:, 512:1024], in0=ps[1][:, :], scalar1=-10.0,
                                  scalar2=10.0, op0=ALU.max, op1=ALU.min).then_inc(s_dv, 1)
                V.dv += 1
            dv_clip = V.dv

            @blk.scalar
            def _(eng, h=h, dv_clip=dv_clip):
                eng.wait_ge(s_dv, dv_clip)
                eng.activation(out=w_h[:, :], in_=w_h[:, :], func=AF.Exp).then_inc(s_ac, 1)
                eng.activation(out=sigr[:, h * T_LOC: h * T_LOC + 512],
                               in_=ps[4][:, :], func=AF.Sigmoid)
                eng.activation(out=sigr[:, h * T_LOC + 512: h * T_LOC + 1024],
                               in_=ps[5][:, :], func=AF.Sigmoid).then_inc(s_ac, 1)
                V.ac += 2
            ac_headsig.append(V.ac)
            ac_w = V.ac - 1

            @blk.vector
            def _(eng, h=h, ac_w=ac_w):
                eng.wait_ge(s_ac, ac_w)
                eng.tensor_tensor(out=w_h[:, 0:1], in0=w_h[:, 0:1],
                                  in1=fe_all[:, h:h + 1], op=ALU.mult)
                eng.drain()
                eng.tensor_tensor(out=a_h[:, 0:512], in0=w_h[:, 0:512],
                                  in1=ps[2][:, :], op=ALU.mult)
                eng.tensor_tensor(out=a_h[:, 512:1024], in0=w_h[:, 512:1024],
                                  in1=ps[3][:, :], op=ALU.mult)
                eng.drain()
                dbc = d_all[:, h:h + 1].broadcast_to([128, T_LOC])
                eng.tensor_tensor_scan(out=scn[:, :], data0=dbc, data1=a_h[:, :],
                                       initial=0.0, op0=ALU.mult, op1=ALU.add)
                eng.tensor_tensor_scan(out=scd[:, :], data0=dbc, data1=w_h[:, :],
                                       initial=0.0, op0=ALU.mult, op1=ALU.add)
                eng.drain()
                eng.tensor_copy(out=numl[:, h * T_LOC:(h + 1) * T_LOC], in_=scn[:, :])
                eng.tensor_copy(out=denl[:, h * T_LOC:(h + 1) * T_LOC], in_=scd[:, :])
                eng.tensor_copy(out=Lo_n[:, h:h + 1], in_=scn[:, T_LOC - 1:T_LOC])
                eng.tensor_copy(out=Lo_d[:, h:h + 1], in_=scd[:, T_LOC - 1:T_LOC]).then_inc(s_dv, 1)
                V.dv += 1
            dv_headev.append(V.dv)

        for h in range(H):
            emit_head(h)
            for g in range(h * 12, min((h + 1) * 12, len(funits))):
                emit_funit_front(g)
                if g >= 1:
                    emit_funit_back(g - 1)
        emit_funit_back(len(funits) - 1)
        sp_wq_done = V.sp

        for nm, src, n_, w_ in (("gwq", sp_gw, I, C), ("dwq", sp_dw, C, I)):
            if nm in dbg:
                @blk.sync
                def _(eng, nm=nm, src=src):
                    eng.wait_ge(s_sp, sp_wq_done)
                    eng.dma_start(out=dbg[nm], in_=src).then_inc(s_out, 16)
                    V.outv += 16

        @blk.gpsimd
        def _(eng):
            eng.wait_ge(s_dv, dv_headev[H - 1])
            eng.dma_start(out=cb_in.ap()[:, 0:H], in_=Lo_n[:, :]).then_inc(s_gp, 16)
            eng.dma_start(out=cb_in.ap()[:, H:2 * H], in_=Lo_d[:, :]).then_inc(s_gp, 16)
            V.gp += 32
            eng.wait_ge(s_gp, V.gp)
            eng.collective_compute(
                "AllGather", ALU.bypass, replica_groups=[list(range(8))],
                ins=[cb_in.ap()], outs=[cb_out.ap()],
            ).then_inc(s_cc, 1)
            V.cc += 1
            eng.wait_ge(s_cc, V.cc)
            eng.dma_start(out=Lg[:, :].rearrange("p (r n) -> p r n", r=8),
                          in_=cb_out.ap().rearrange("r p n -> p r n")).then_inc(s_gp, 16)
            V.gp += 16
        gp_carry = V.gp

        @blk.vector
        def _(eng):
            eng.wait_ge(s_gp, gp_carry)
            eng.memset(car_n[:, :], 0.0)
            eng.memset(car_d[:, :], 0.0)
            eng.nop().then_inc(s_dv, 1)
            V.dv += 1
        for j in range(8):
            dv_prev = V.dv

            @blk.vector
            def _(eng, j=j, dv_prev=dv_prev):
                eng.wait_ge(s_dv, dv_prev)
                eng.wait_ge(s_ac, V.ac)
                eng.tensor_scalar(out=tt16[:, :], in0=e_all[:, :],
                                  scalar1=cws[:, j:j + 1], scalar2=None,
                                  op0=ALU.mult).then_inc(s_dv, 1)
                V.dv += 1
            dv_t = V.dv

            @blk.scalar
            def _(eng, dv_t=dv_t):
                eng.wait_ge(s_dv, dv_t)
                eng.activation(out=co16[:, :], in_=tt16[:, :], func=AF.Exp,
                               scale=-1.0).then_inc(s_ac, 1)
                V.ac += 1
            ac_c = V.ac

            @blk.vector
            def _(eng, j=j, ac_c=ac_c):
                eng.wait_ge(s_ac, ac_c)
                eng.tensor_tensor(out=tt16[:, :], in0=co16[:, :],
                                  in1=Lg[:, j * 2 * H: j * 2 * H + H], op=ALU.mult)
                eng.drain()
                eng.tensor_tensor(out=car_n[:, :], in0=car_n[:, :], in1=tt16[:, :],
                                  op=ALU.add)
                eng.drain()
                eng.tensor_tensor(out=tt16[:, :], in0=co16[:, :],
                                  in1=Lg[:, j * 2 * H + H: (j + 1) * 2 * H], op=ALU.mult)
                eng.drain()
                eng.tensor_tensor(out=car_d[:, :], in0=car_d[:, :], in1=tt16[:, :],
                                  op=ALU.add).then_inc(s_dv, 1)
                V.dv += 1
        dv_carry = V.dv

        for h in range(H):
            @blk.vector
            def _(eng, h=h):
                eng.wait_ge(s_dv, dv_carry)
                eng.wait_ge(s_ac, V.ac)
                dbc = d_all[:, h:h + 1].broadcast_to([128, T_LOC])
                zbc = zero1[:, 0:1].broadcast_to([128, T_LOC])
                eng.tensor_tensor_scan(out=scn[:, :], data0=dbc, data1=zbc,
                                       initial=1.0, op0=ALU.mult, op1=ALU.add)
                eng.drain()
                eng.scalar_tensor_tensor(out=scd[:, :], in0=scn[:, :],
                                         scalar=car_d[:, h:h + 1],
                                         in1=denl[:, h * T_LOC:(h + 1) * T_LOC],
                                         op0=ALU.mult, op1=ALU.add)
                eng.scalar_tensor_tensor(out=scn[:, :], in0=scn[:, :],
                                         scalar=car_n[:, h:h + 1],
                                         in1=numl[:, h * T_LOC:(h + 1) * T_LOC],
                                         op0=ALU.mult, op1=ALU.add).then_inc(s_dv, 1)
                V.dv += 1
            dv_nf = V.dv

            @blk.scalar
            def _(eng, dv_nf=dv_nf):
                eng.wait_ge(s_dv, dv_nf)
                eng.activation(out=scd[:, :], in_=scd[:, :], func=AF.Ln,
                               bias=ceps6[:, 0:1])
                eng.drain()
                eng.activation(out=scd[:, :], in_=scd[:, :], func=AF.Exp,
                               scale=-1.0).then_inc(s_ac, 1)
                V.ac += 1
            ac_rd = V.ac

            @blk.vector
            def _(eng, h=h, ac_rd=ac_rd):
                eng.wait_ge(s_ac, ac_rd)
                eng.tensor_tensor(out=scn[:, :], in0=scn[:, :], in1=scd[:, :],
                                  op=ALU.mult)
                eng.drain()
                eng.tensor_tensor(out=numl[:, h * T_LOC:(h + 1) * T_LOC],
                                  in0=scn[:, :],
                                  in1=sigr[:, h * T_LOC:(h + 1) * T_LOC],
                                  op=ALU.mult).then_inc(s_dv, 1)
                V.dv += 1
        dv_rwkv = V.dv

        @blk.scalar
        def _(eng):
            eng.wait_ge(s_dv, dv_rwkv)
            for ct in range(NCT):
                eng.dma_start(out=sp_rw[ct * 128:(ct + 1) * 128, :],
                              in_=numl[:, ct * T_LOC:(ct + 1) * T_LOC]).then_inc(s_sp, 16)
                V.sp += 16
        sp_rwd = V.sp

        if "rwkvT" in dbg:
            @blk.sync
            def _(eng):
                eng.wait_ge(s_dv, dv_rwkv)
                for ct in range(NCT):
                    eng.dma_start(out=dbg["rwkvT"][ct * 128:(ct + 1) * 128, :],
                                  in_=numl[:, ct * T_LOC:(ct + 1) * T_LOC]).then_inc(s_out, 16)
                    V.outv += 16

        free("rsF3", "rsF2", "rsF1", "rsF0", "s2F3", "s2F2", "s2F1", "s2F0", "bmF3", "bmF2", "bmF1", "bmF0",
             "q8F1", "q8F0", "qoutF1", "qoutF0", "natF3", "natF2", "natF1", "natF0",
             "a_h", "w_h",
             "wrt0", "wvt0", "wkt0", "xnT",
             "sigr", "denl", "numl", "scd", "scn")
        barrier()

        xqT = sb("xqT", [128, NCT * T_LOC], BF16)
        gwqA = sb("gwqA", [128, NCT * 512], BF16)
        uwqA = sb("uwqA", [128, NCT * 512], BF16)
        gwqB = sb("gwqB", [128, NCT * 512], BF16)
        uwqB = sb("uwqB", [128, NCT * 512], BF16)
        sg512 = sb("sg512", [128, 512], F32)
        h512 = sb("h512", [128, 512], F32)
        q8h = sb("q8h", [128, 512], FP8)
        hstage = sb("hstage", [128, NT * 512], BF16)

        @blk.sync
        def _(eng):
            for ct in range(NCT):
                eng.dma_start_transpose(
                    out=xqT[:, ct * T_LOC:(ct + 1) * T_LOC],
                    in_=sp_xq[:, ct * 128:(ct + 1) * 128]).then_inc(s_tr, 16)
                V.tr += 16
        tr_xq = V.tr

        NICH = I // 512
        tr_hch = []
        pe_chunk = []
        dv_hq = []
        sp_hch = []
        for ic in range(NICH):
            gq_b, uq_b = ((gwqA, uwqA), (gwqB, uwqB))[ic % 2]

            @blk.sync
            def _(eng, ic=ic, gq_b=gq_b, uq_b=uq_b):
                if ic >= 2:
                    eng.wait_ge(s_pe, pe_chunk[ic - 2])
                for ct in range(NCT):
                    eng.dma_start_transpose(
                        out=gq_b[:, ct * 512:(ct + 1) * 512],
                        in_=sp_gw[ic * 512:(ic + 1) * 512, ct * 128:(ct + 1) * 128],
                    ).then_inc(s_tr, 16)
                    eng.dma_start_transpose(
                        out=uq_b[:, ct * 512:(ct + 1) * 512],
                        in_=sp_uw[ic * 512:(ic + 1) * 512, ct * 128:(ct + 1) * 128],
                    ).then_inc(s_tr, 16)
                    V.tr += 32
            tr_hch.append(V.tr)

            for tt in range(NT):
                bk_g = ps[(tt % 2) * 2]
                bk_u = ps[(tt % 2) * 2 + 1]
                idx = ic * NT + tt

                @blk.tensor
                def _(eng, ic=ic, tt=tt, gq_b=gq_b, uq_b=uq_b, bk_g=bk_g,
                      bk_u=bk_u, idx=idx):
                    eng.wait_ge(s_tr, tr_hch[ic])
                    if idx >= 2:
                        eng.wait_ge(s_dv, dv_hq[idx - 2])
                    for ci in range(NCT):
                        eng.matmul(bk_g[:, :],
                                   xqT[:, ci * T_LOC + tt * 128: ci * T_LOC + tt * 128 + 128],
                                   gq_b[:, ci * 512:(ci + 1) * 512],
                                   start=(ci == 0), stop=(ci == NCT - 1))
                    mm = None
                    for ci in range(NCT):
                        mm = eng.matmul(bk_u[:, :],
                                        xqT[:, ci * T_LOC + tt * 128: ci * T_LOC + tt * 128 + 128],
                                        uq_b[:, ci * 512:(ci + 1) * 512],
                                        start=(ci == 0), stop=(ci == NCT - 1))
                    mm.then_inc(s_pe, 1)
                    V.pe += 1
                pe_gu = V.pe

                @blk.scalar
                def _(eng, pe_gu=pe_gu, bk_g=bk_g):
                    eng.wait_ge(s_pe, pe_gu)
                    eng.wait_ge(s_dv, V.dv)
                    eng.activation(out=sg512[:, :], in_=bk_g[:, :],
                                   func=AF.Silu).then_inc(s_ac, 1)
                    V.ac += 1
                ac_sg = V.ac

                @blk.vector
                def _(eng, tt=tt, ac_sg=ac_sg, bk_u=bk_u, ic=ic):
                    eng.wait_ge(s_ac, ac_sg)
                    if sp_hch and tt == 0:
                        eng.wait_ge(s_sp, sp_hch[-1])
                    eng.tensor_tensor(out=h512[:, :], in0=sg512[:, :], in1=bk_u[:, :],
                                      op=ALU.mult)
                    eng.drain()
                    eng.tensor_reduce(out=bm64[:, 0:16],
                                      in_=h512[:, :].rearrange("p (b k) -> p b k", k=32),
                                      axis=AX.X, op=ALU.max, apply_absolute_value=True)
                    eng.drain()
                    eng.tensor_scalar(out=s2x[:, 0:16], in0=bm64[:, 0:16], scalar1=R448_2,
                                      scalar2=2e-30, op0=ALU.mult, op1=ALU.max)
                    eng.drain()
                    eng.reciprocal(out=rs2[:, 0:16], in_=s2x[:, 0:16])
                    eng.drain()
                    eng.tensor_tensor(out=q8h[:, :].rearrange("p (b k) -> p b k", k=32),
                                      in0=h512[:, :].rearrange("p (b k) -> p b k", k=32),
                                      in1=rs2[:, 0:16].rearrange("p (b o) -> p b o", o=1
                                          ).broadcast_to([128, 16, 32]),
                                      op=ALU.mult)
                    eng.drain()
                    eng.tensor_tensor(out=hstage[:, tt * 512:(tt + 1) * 512
                                      ].rearrange("p (b k) -> p b k", k=32),
                                      in0=q8h[:, :].rearrange("p (b k) -> p b k", k=32),
                                      in1=s2x[:, 0:16].rearrange("p (b o) -> p b o", o=1
                                          ).broadcast_to([128, 16, 32]),
                                      op=ALU.mult).then_inc(s_dv, 1)
                    V.dv += 1
                dv_hq.append(V.dv)
            pe_chunk.append(V.pe)

            @blk.scalar
            def _(eng, ic=ic):
                eng.wait_ge(s_dv, dv_hq[-1])
                eng.dma_start(
                    out=sp_hq.rearrange("(t p) i -> p t i", p=128)[:, :, ic * 512:(ic + 1) * 512],
                    in_=hstage[:, :].rearrange("p (t n) -> p t n", t=NT),
                ).then_inc(s_sp, 16)
                V.sp += 16
            sp_hch.append(V.sp)
        sp_hqd = V.sp

        if "hq" in dbg:
            @blk.sync
            def _(eng):
                eng.wait_ge(s_sp, sp_hqd)
                eng.dma_start(out=dbg["hq"], in_=sp_hq).then_inc(s_out, 16)
                V.outv += 16

        free("hstage", "q8h", "h512", "sg512", "uwqB", "gwqB", "uwqA", "gwqA",
             "xqT")
        barrier()

        acc = sb("acc", [128, NT * C], F32)
        hqgA = sb("hqgA", [128, 8 * T_LOC], BF16)
        hqgB = sb("hqgB", [128, 8 * T_LOC], BF16)
        dwqA = sb("dwqA", [128, 8 * C], BF16)
        dwqB = sb("dwqB", [128, 8 * C], BF16)

        @blk.vector
        def _(eng):
            eng.memset(acc[:, :], 0.0)
            eng.nop().then_inc(s_dv, 1)
            V.dv += 1
        dv_accz = V.dv

        NG8 = 8
        tr_g8 = []
        pe_g8 = []
        dv_unit = []
        pe_unit = []
        for g8 in range(NG8):
            hq_b = (hqgA, hqgB)[g8 % 2]
            dq_b = (dwqA, dwqB)[g8 % 2]

            @blk.sync
            def _(eng, g8=g8, hq_b=hq_b, dq_b=dq_b):
                if g8 >= 2:
                    eng.wait_ge(s_pe, pe_g8[g8 - 2])
                for k in range(8):
                    it = g8 * 8 + k
                    eng.dma_start_transpose(
                        out=hq_b[:, k * T_LOC:(k + 1) * T_LOC],
                        in_=sp_hq[:, it * 128:(it + 1) * 128]).then_inc(s_tr, 16)
                    eng.dma_start_transpose(
                        out=dq_b[:, k * C:(k + 1) * C],
                        in_=sp_dw[:, it * 128:(it + 1) * 128]).then_inc(s_tr, 16)
                    V.tr += 32
            tr_g8.append(V.tr)

            for u in range(8):
                ncn = u // 2
                qd = u % 2
                half = (u % 2)
                gu = g8 * 8 + u

                @blk.tensor
                def _(eng, g8=g8, u=u, ncn=ncn, qd=qd, half=half, gu=gu,
                      hq_b=hq_b, dq_b=dq_b):
                    eng.wait_ge(s_tr, tr_g8[g8])
                    if gu == 0:
                        eng.wait_ge(s_dv, dv_accz)
                    if gu >= 2:
                        eng.wait_ge(s_dv, dv_unit[gu - 2])
                    mm = None
                    for tti in range(4):
                        tt = qd * 4 + tti
                        for k in range(8):
                            mm = eng.matmul(ps[half * 4 + tti][:, :],
                                            hq_b[:, k * T_LOC + tt * 128: k * T_LOC + tt * 128 + 128],
                                            dq_b[:, k * C + ncn * 512: k * C + ncn * 512 + 512],
                                            start=(k == 0), stop=(k == 7))
                    mm.then_inc(s_pe, 1)
                    V.pe += 1
                pe_unit.append(V.pe)

                @blk.vector
                def _(eng, ncn=ncn, qd=qd, half=half, pe_u=V.pe):
                    eng.wait_ge(s_pe, pe_u)
                    for tti in range(4):
                        tt = qd * 4 + tti
                        sl = slice(tt * C + ncn * 512, tt * C + ncn * 512 + 512)
                        eng.tensor_tensor(out=acc[:, sl], in0=acc[:, sl],
                                          in1=ps[half * 4 + tti][:, :], op=ALU.add)
                    eng.nop().then_inc(s_dv, 1)
                    V.dv += 1
                dv_unit.append(V.dv)
            pe_g8.append(V.pe)
        dv_down = V.dv

        free("dwqB", "dwqA", "hqgB", "hqgA")
        barrier()

        rwT = sb("rwT", [128, NCT * T_LOC], BF16)
        wotc = sb("wotc", [128, NCT * 512], BF16)
        xbufJ = sb("xbufJ", [128, NT * C], F32)
        gsil = sb("gsil", [128, C], F32)
        otile = sb("otile", [128, 2 * 512], F32)

        @blk.scalar
        def _(eng):
            eng.wait_ge(s_sp, sp_rwd)
            eng.dma_start(out=rwT[:, :].rearrange("p (c n) -> p c n", c=NCT),
                          in_=sp_rw.rearrange("(c p) n -> p c n", p=128)).then_inc(s_ld, 16)
            eng.dma_start(out=xbufJ[:, :].rearrange("p (t c) -> p t c", t=NT),
                          in_=x_tm.rearrange("(t p) c -> p t c", p=128)).then_inc(s_ld, 16)
            V.ld += 32
        ld_rw = V.ld

        gp_wot = []
        pe_s = []
        dv_fin = []
        for ncn in range(4):
            @blk.gpsimd
            def _(eng, ncn=ncn):
                if ncn >= 1:
                    eng.wait_ge(s_pe, pe_s[-1])
                eng.dma_start(out=wotc[:, :].rearrange("p (c n) -> p c n", c=NCT),
                              in_=WoT.rearrange("(c p) n -> p c n", p=128)[:, :, ncn * 512:(ncn + 1) * 512],
                              ).then_inc(s_gp, 16)
                V.gp += 16
            gp_wot.append(V.gp)

            for tt in range(NT):
                bank = ps[tt % 4]
                idx = ncn * NT + tt

                @blk.tensor
                def _(eng, ncn=ncn, tt=tt, bank=bank, idx=idx):
                    eng.wait_ge(s_gp, gp_wot[ncn])
                    eng.wait_ge(s_ld, ld_rw)
                    if idx >= 4:
                        eng.wait_ge(s_dv, dv_fin[idx - 4])
                    mm = None
                    for ci in range(NCT):
                        mm = eng.matmul(bank[:, :],
                                        rwT[:, ci * T_LOC + tt * 128: ci * T_LOC + tt * 128 + 128],
                                        wotc[:, ci * 512:(ci + 1) * 512],
                                        start=(ci == 0), stop=(ci == NCT - 1))
                    mm.then_inc(s_pe, 1)
                    V.pe += 1
                pe_s.append(V.pe)
                pe_this = V.pe

                @blk.scalar
                def _(eng, ncn=ncn, tt=tt, pe_this=pe_this):
                    eng.wait_ge(s_pe, pe_this)
                    eng.wait_ge(s_dv, V.dv)
                    eng.activation(out=gsil[:, ncn * 512:(ncn + 1) * 512],
                                   in_=acc[:, tt * C + ncn * 512: tt * C + ncn * 512 + 512],
                                   func=AF.Silu).then_inc(s_ac, 1)
                    V.ac += 1
                ac_g = V.ac

                slot = slice((idx % 2) * 512, (idx % 2) * 512 + 512)
                out_done = V.outv

                @blk.vector
                def _(eng, ncn=ncn, tt=tt, ac_g=ac_g, bank=bank, slot=slot,
                      out_done=out_done, idx=idx):
                    eng.wait_ge(s_ac, ac_g)
                    if idx >= 2:
                        eng.wait_ge(s_out, out_done - 16)
                    gl = slice(ncn * 512, (ncn + 1) * 512)
                    eng.tensor_tensor(out=otile[:, slot], in0=bank[:, :],
                                      in1=gsil[:, gl], op=ALU.mult)
                    eng.drain()
                    eng.tensor_tensor(out=otile[:, slot], in0=otile[:, slot],
                                      in1=xbufJ[:, tt * C + ncn * 512: tt * C + ncn * 512 + 512],
                                      op=ALU.add).then_inc(s_dv, 1)
                    V.dv += 1
                dv_fin.append(V.dv)

                @blk.scalar
                def _(eng, ncn=ncn, tt=tt, slot=slot):
                    eng.wait_ge(s_dv, dv_fin[-1])
                    eng.dma_start(out=out[tt * 128:(tt + 1) * 128, ncn * 512:(ncn + 1) * 512],
                                  in_=otile[:, slot]).then_inc(s_out, 16)
                    V.outv += 16

        @blk.scalar
        def _(eng):
            eng.wait_ge(s_out, V.outv)

    for n in reversed(list(live)):
        live[n].__exit__(None, None, None)
    for cm in reversed(ctx):
        cm.__exit__(None, None, None)
    return nc


def make_inmaps(inputs):
    x = np.asarray(inputs["L0_attn_x"], dtype=np.float32).reshape(8192, 2048)
    WkT = np.ascontiguousarray(np.asarray(inputs["L0_attn_Wk"]).T)
    WvT = np.ascontiguousarray(np.asarray(inputs["L0_attn_Wv"]).T)
    WrT = np.ascontiguousarray(np.asarray(inputs["L0_attn_Wr"]).T)
    WoT = np.ascontiguousarray(np.asarray(inputs["L0_attn_Wo"]).T)
    gw = np.ascontiguousarray(np.asarray(inputs["L0_ffn_gate_w"], dtype=np.float32))
    uw = np.ascontiguousarray(np.asarray(inputs["L0_ffn_up_w"], dtype=np.float32))
    dw = np.ascontiguousarray(np.asarray(inputs["L0_ffn_down_w"], dtype=np.float32))
    tdT = np.ascontiguousarray(np.asarray(inputs["L0_attn_time_decay"]).T.astype(np.float32))
    tfT = np.ascontiguousarray(np.asarray(inputs["L0_attn_time_first"]).T.astype(np.float32))
    maps = []
    for i in range(8):
        cw_row = np.zeros(8, np.float32)
        for j in range(8):
            cw_row[j] = 1024.0 * (i - 1 - j) if j < i else 1e9
        maps.append({
            "x_tm": np.ascontiguousarray(x[i * 1024:(i + 1) * 1024]),
            "WkT": WkT, "WvT": WvT, "WrT": WrT, "WoT": WoT,
            "gw": gw, "uw": uw, "dw": dw,
            "tdT": tdT, "tfT": tfT,
            "cwi": np.tile(cw_row, (128, 1)),
            "c0f": np.full((128, 1), 1.0 if i == 0 else 0.0, np.float32),
        })
    return maps


def kernel(**inputs):
    nc = build_nc()
    maps = make_inmaps(inputs)
    res = run_bass_kernel_spmd(nc, maps, core_ids=list(range(8)))
    o = np.concatenate([np.asarray(r["out"]) for r in res.results], axis=0)
    return o.reshape(1, 8192, 2048).astype(np.float32)


# version 05
# speedup vs baseline: 1.2295x; 1.2295x over previous
import numpy as np
import concourse.bass as bass
import concourse.mybir as mybir
from concourse.bass_utils import run_bass_kernel_spmd

F32 = mybir.dt.float32
BF16 = mybir.dt.bfloat16
FP8 = mybir.dt.float8e4
ALU = mybir.AluOpType
AF = mybir.ActivationFunctionType
AX = mybir.AxisListType

T_LOC = 1024
C = 2048
H = 16
S = 128
I = 8192
NT = T_LOC // 128
NCT = C // 128
NIT = I // 128

R448 = float(np.float32(1.0) / np.float32(448.0))
R448_2 = float(np.float32(2.0) * (np.float32(1.0) / np.float32(448.0)))


def build_nc(debug=()):
    nc = bass.Bass("TRN2", target_bir_lowering=False, debug=False,
                   detect_race_conditions=False)
    x_tm = nc.dram_tensor("x_tm", [T_LOC, C], F32, kind="ExternalInput").ap()
    WkT = nc.dram_tensor("WkT", [C, C], F32, kind="ExternalInput").ap()
    WvT = nc.dram_tensor("WvT", [C, C], F32, kind="ExternalInput").ap()
    WrT = nc.dram_tensor("WrT", [C, C], F32, kind="ExternalInput").ap()
    WoT = nc.dram_tensor("WoT", [C, C], F32, kind="ExternalInput").ap()
    gw = nc.dram_tensor("gw", [I, C], F32, kind="ExternalInput").ap()
    uw = nc.dram_tensor("uw", [I, C], F32, kind="ExternalInput").ap()
    dw = nc.dram_tensor("dw", [C, I], F32, kind="ExternalInput").ap()
    tdT = nc.dram_tensor("tdT", [S, H], F32, kind="ExternalInput").ap()
    tfT = nc.dram_tensor("tfT", [S, H], F32, kind="ExternalInput").ap()
    cwi = nc.dram_tensor("cwi", [S, 8], F32, kind="ExternalInput").ap()
    c0f = nc.dram_tensor("c0f", [S, 1], F32, kind="ExternalInput").ap()
    out = nc.dram_tensor("out", [T_LOC, C], F32, kind="ExternalOutput").ap()
    sp_xn = nc.dram_tensor("sp_xn", [T_LOC, C], BF16).ap()
    sp_xq = nc.dram_tensor("sp_xq", [T_LOC, C], BF16).ap()
    sp_gw = nc.dram_tensor("sp_gw", [I, C], BF16).ap()
    sp_uw = nc.dram_tensor("sp_uw", [I, C], BF16).ap()
    sp_dw = nc.dram_tensor("sp_dw", [C, I], BF16).ap()
    sp_hq = nc.dram_tensor("sp_hq", [T_LOC, I], BF16).ap()
    sp_rw = nc.dram_tensor("sp_rw", [C, T_LOC], BF16).ap()
    cb_in = nc.dram_tensor("cb_in", [S, 2 * H], F32)
    cb_out = nc.dram_tensor("cb_out", [8, S, 2 * H], F32)

    dbg = {}
    for name, shape, dt in (
        ("rwkvT", [C, T_LOC], BF16), ("gwq", [I, C], BF16),
        ("dwq", [C, I], BF16), ("hq", [T_LOC, I], BF16),
    ):
        if name in debug:
            dbg[name] = nc.dram_tensor("dbg_" + name, shape, dt,
                                       kind="ExternalOutput").ap()

    live = {}

    def sb(name, shape, dt):
        cm = nc.sbuf_tensor(name, shape, dt)
        t = cm.__enter__()
        live[name] = cm
        return t

    def free(*names):
        for n in names:
            live.pop(n).__exit__(None, None, None)

    ctx = []

    def ps_(name):
        cm = nc.psum_tensor(name, [128, 512], F32)
        t = cm.__enter__()
        ctx.append(cm)
        return t

    def semi(name):
        cm = nc.semaphore(name)
        s = cm.__enter__()
        ctx.append(cm)
        return s

    stat = sb("stat", [128, 16], F32)
    bm64 = sb("bm64", [128, 64], F32)
    s2x = sb("s2x", [128, 64], F32)
    rs2 = sb("rs2", [128, 64], F32)
    ceps5 = sb("ceps5", [128, 1], F32)
    ceps6 = sb("ceps6", [128, 1], F32)
    zero1 = sb("zero1", [128, 1], F32)
    tds = sb("tds", [128, H], F32)
    tfs = sb("tfs", [128, H], F32)
    cws = sb("cws", [128, 8], F32)
    c0s = sb("c0s", [128, 1], F32)
    e_all = sb("e_all", [128, H], F32)
    d_all = sb("d_all", [128, H], F32)
    fe_all = sb("fe_all", [128, H], F32)
    Lo_n = sb("Lo_n", [128, H], F32)
    Lo_d = sb("Lo_d", [128, H], F32)
    Lg = sb("Lg", [128, 8 * 2 * H], F32)
    car_n = sb("car_n", [128, H], F32)
    car_d = sb("car_d", [128, H], F32)
    co16 = sb("co16", [128, H], F32)
    tt16 = sb("tt16", [128, H], F32)

    ps = [ps_("psb%d" % i) for i in range(8)]

    s_ld = semi("s_ld")
    s_gp = semi("s_gp")
    s_tr = semi("s_tr")
    s_sp = semi("s_sp")
    s_pe = semi("s_pe")
    s_dv = semi("s_dv")
    s_ac = semi("s_ac")
    s_gc = semi("s_gc")
    s_cc = semi("s_cc")
    s_out = semi("s_out")

    class V:
        ld = gp = tr = sp = pe = dv = ac = gc = cc = outv = 0

    with nc.Block() as blk:

        def barrier():
            pairs = [(s_ld, V.ld), (s_gp, V.gp), (s_tr, V.tr), (s_sp, V.sp),
                     (s_pe, V.pe), (s_dv, V.dv), (s_ac, V.ac), (s_gc, V.gc)]
            for deco in (blk.sync, blk.scalar, blk.vector, blk.gpsimd, blk.tensor):
                @deco
                def _(eng, pairs=pairs):
                    for sem, val in pairs:
                        if val:
                            eng.wait_ge(sem, val)

        scn = sb("scn", [128, T_LOC], F32)
        scd = sb("scd", [128, T_LOC], F32)
        numl = sb("numl", [128, H * T_LOC], BF16)
        denl = sb("denl", [128, H * T_LOC], BF16)
        sigr = sb("sigr", [128, H * T_LOC], BF16)

        xbuf = sb("xbuf", [128, NT * C], F32)
        xn32 = sb("xn32", [128, C], F32)
        xc32 = sb("xc32", [128, C], F32)
        sq32 = sb("sq32", [128, C], F32)
        xnbf = sb("xnbf", [128, C], BF16)
        xqbf = sb("xqbf", [128, C], BF16)
        q8a = sb("q8a", [128, C], FP8)

        @blk.scalar
        def _(eng):
            eng.dma_start(out=xbuf[:, :].rearrange("p (t c) -> p t c", t=NT),
                          in_=x_tm.rearrange("(t p) c -> p t c", p=128)).then_inc(s_ld, 16)
            V.ld += 16
        ld_x = V.ld

        @blk.gpsimd
        def _(eng):
            eng.dma_start(out=tds[:, :], in_=tdT).then_inc(s_gp, 16)
            eng.dma_start(out=tfs[:, :], in_=tfT).then_inc(s_gp, 16)
            eng.dma_start(out=cws[:, :], in_=cwi).then_inc(s_gp, 16)
            eng.dma_start(out=c0s[:, :], in_=c0f).then_inc(s_gp, 16)
            V.gp += 64
        gp_const = V.gp

        @blk.vector
        def _(eng):
            eng.memset(ceps5[:, :], 1e-5)
            eng.memset(ceps6[:, :], 1e-6)
            eng.memset(zero1[:, :], 0.0)
            eng.nop().then_inc(s_dv, 1)
            V.dv += 1

        for tt in range(NT):
            xt = xbuf[:, tt * C:(tt + 1) * C]

            @blk.vector
            def _(eng, xt=xt):
                eng.wait_ge(s_ld, ld_x)
                eng.wait_ge(s_ac, V.ac)
                eng.wait_ge(s_sp, V.sp)
                eng.tensor_reduce(out=stat[:, 0:1], in_=xt, axis=AX.X, op=ALU.add,
                                  negate=True)
                eng.drain()
                eng.tensor_scalar(out=stat[:, 1:2], in0=stat[:, 0:1],
                                  scalar1=float(1.0 / C), scalar2=None, op0=ALU.mult)
                eng.drain()
                eng.tensor_scalar(out=xc32[:, :], in0=xt, scalar1=stat[:, 1:2],
                                  scalar2=None, op0=ALU.add)
                eng.drain()
                eng.scalar_tensor_tensor(out=sq32[:, :], in0=xc32[:, :], scalar=1.0,
                                         in1=xc32[:, :], op0=ALU.mult, op1=ALU.mult,
                                         accum_out=stat[:, 2:3]).then_inc(s_dv, 1)
                V.dv += 1
            dv_xc = V.dv

            @blk.scalar
            def _(eng, dv_xc=dv_xc):
                eng.wait_ge(s_dv, dv_xc)
                eng.activation(out=stat[:, 3:4], in_=stat[:, 2:3], func=AF.Sqrt,
                               scale=float(1.0 / C), bias=ceps5[:, 0:1]).then_inc(s_ac, 1)
                V.ac += 1
            ac_std = V.ac

            @blk.vector
            def _(eng, ac_std=ac_std):
                eng.wait_ge(s_ac, ac_std)
                eng.reciprocal(out=stat[:, 4:5], in_=stat[:, 3:4])
                eng.drain()
                eng.tensor_scalar(out=xn32[:, :], in0=xc32[:, :], scalar1=stat[:, 4:5],
                                  scalar2=None, op0=ALU.mult)
                eng.drain()
                eng.tensor_copy(out=xnbf[:, :], in_=xn32[:, :])
                eng.tensor_reduce(out=bm64[:, 0:64],
                                  in_=xn32[:, :].rearrange("p (b k) -> p b k", k=32),
                                  axis=AX.X, op=ALU.max, apply_absolute_value=True)
                eng.drain()
                eng.tensor_scalar(out=s2x[:, 0:64], in0=bm64[:, 0:64], scalar1=R448_2,
                                  scalar2=2e-30, op0=ALU.mult, op1=ALU.max)
                eng.drain()
                eng.reciprocal(out=rs2[:, 0:64], in_=s2x[:, 0:64])
                eng.drain()
                eng.tensor_tensor(out=q8a[:, :].rearrange("p (b k) -> p b k", k=32),
                                  in0=xn32[:, :].rearrange("p (b k) -> p b k", k=32),
                                  in1=rs2[:, 0:64].rearrange("p (b o) -> p b o", o=1
                                      ).broadcast_to([128, 64, 32]),
                                  op=ALU.mult)
                eng.drain()
                eng.tensor_tensor(out=xqbf[:, :].rearrange("p (b k) -> p b k", k=32),
                                  in0=q8a[:, :].rearrange("p (b k) -> p b k", k=32),
                                  in1=s2x[:, 0:64].rearrange("p (b o) -> p b o", o=1
                                      ).broadcast_to([128, 64, 32]),
                                  op=ALU.mult).then_inc(s_dv, 1)
                V.dv += 1
            dv_xq = V.dv

            @blk.scalar
            def _(eng, tt=tt, dv_xq=dv_xq):
                eng.wait_ge(s_dv, dv_xq)
                eng.dma_start(out=sp_xn[tt * 128:(tt + 1) * 128, :],
                              in_=xnbf[:, :]).then_inc(s_sp, 16)
                eng.dma_start(out=sp_xq[tt * 128:(tt + 1) * 128, :],
                              in_=xqbf[:, :]).then_inc(s_sp, 16)
                V.sp += 32
        sp_xnxq = V.sp
        free("q8a", "xqbf", "xnbf", "sq32", "xc32", "xn32", "xbuf")

        xnT = sb("xnT", [128, NCT * T_LOC], BF16)
        wbufs = [(sb("wkt0", [128, NCT * 128], BF16),
                  sb("wvt0", [128, NCT * 128], BF16),
                  sb("wrt0", [128, NCT * 128], BF16))] * 2
        w_h = sb("w_h", [128, T_LOC], F32)
        a_h = sb("a_h", [128, T_LOC], F32)
        natF = [sb("natF%d" % i, [128, C], F32) for i in range(4)]
        qoutF = [sb("qoutF%d" % i, [128, C], BF16) for i in range(2)]
        q8F = [sb("q8F%d" % i, [128, C], FP8) for i in range(2)]
        bmF = [sb("bmF%d" % i, [128, 64], F32) for i in range(4)]
        s2F = [sb("s2F%d" % i, [128, 64], F32) for i in range(4)]
        rsF = [sb("rsF%d" % i, [128, 64], F32) for i in range(4)]

        @blk.sync
        def _(eng):
            eng.wait_ge(s_sp, sp_xnxq)
            for ct in range(NCT):
                eng.dma_start_transpose(
                    out=xnT[:, ct * T_LOC:(ct + 1) * T_LOC],
                    in_=sp_xn[:, ct * 128:(ct + 1) * 128]).then_inc(s_tr, 16)
                V.tr += 16
        tr_xn = V.tr

        @blk.scalar
        def _(eng):
            eng.wait_ge(s_gp, gp_const)
            eng.activation(out=e_all[:, :], in_=tds[:, :], func=AF.Exp)
            eng.drain()
            eng.activation(out=d_all[:, :], in_=e_all[:, :], func=AF.Exp, scale=-1.0)
            eng.activation(out=fe_all[:, :], in_=tfs[:, :], func=AF.Exp).then_inc(s_ac, 1)
            V.ac += 1
        ac_decay = V.ac

        @blk.vector
        def _(eng):
            eng.wait_ge(s_ac, ac_decay)
            eng.scalar_tensor_tensor(out=fe_all[:, :], in0=fe_all[:, :], scalar=-1.0,
                                     in1=c0s[:, 0:1].broadcast_to([128, H]),
                                     op0=ALU.add, op1=ALU.mult)
            eng.drain()
            eng.tensor_scalar(out=fe_all[:, :], in0=fe_all[:, :], scalar1=1.0,
                              scalar2=None, op0=ALU.add).then_inc(s_dv, 1)
            V.dv += 1
        dv_fe = V.dv

        funits = []
        for it in range(NIT):
            funits.append((gw, sp_gw, it, 0))
        for it in range(NIT):
            funits.append((uw, sp_uw, it, 0))
        for ct in range(NCT):
            for cs in range(4):
                funits.append((dw, sp_dw, ct, cs * C))

        gc_wq = []
        dv_sc_l = []
        dv_wq = []
        sp_wq = []

        def emit_funit_front(g):
            src, dst, rt, co = funits[g]
            p = g % 4
            nat = natF[p]
            bmp, s2p, rsp = bmF[p], s2F[p], rsF[p]

            @blk.sync
            def _(eng, g=g, nat=nat, src=src, rt=rt, co=co):
                if g >= 4:
                    eng.wait_ge(s_gc, gc_wq[g - 4])
                eng.dma_start(out=nat[:, :],
                              in_=src[rt * 128:(rt + 1) * 128, co:co + C]).then_inc(s_ld, 16)
                V.ld += 16
            ld_nat = V.ld

            @blk.vector
            def _(eng, g=g, ld_nat=ld_nat, nat=nat, bmp=bmp, s2p=s2p, rsp=rsp):
                eng.wait_ge(s_ld, ld_nat)
                if g >= 4:
                    eng.wait_ge(s_gc, gc_wq[g - 4])
                eng.tensor_reduce(out=bmp[:, :],
                                  in_=nat[:, :].rearrange("p (b k) -> p b k", k=32),
                                  axis=AX.X, op=ALU.max, apply_absolute_value=True)
                eng.drain()
                eng.tensor_scalar(out=s2p[:, :], in0=bmp[:, :], scalar1=R448_2,
                                  scalar2=2e-30, op0=ALU.mult, op1=ALU.max)
                eng.drain()
                eng.reciprocal(out=rsp[:, :], in_=s2p[:, :]).then_inc(s_dv, 1)
                V.dv += 1
            dv_sc_l.append(V.dv)

            @blk.gpsimd
            def _(eng, g=g, nat=nat, rsp=rsp):
                q8w = q8F[g % 2]
                eng.wait_ge(s_dv, dv_sc_l[g])
                if g >= 2:
                    eng.wait_ge(s_dv, dv_wq[g - 2])
                eng.tensor_tensor(out=q8w[:, :].rearrange("p (b k) -> p b k", k=32),
                                  in0=nat[:, :].rearrange("p (b k) -> p b k", k=32),
                                  in1=rsp[:, :].rearrange("p (b o) -> p b o", o=1
                                      ).broadcast_to([128, 64, 32]),
                                  op=ALU.mult).then_inc(s_gc, 1)
                V.gc += 1
            gc_wq.append(V.gc)

        def emit_funit_back(g):
            src, dst, rt, co = funits[g]
            qout, q8w, s2p = qoutF[g % 2], q8F[g % 2], s2F[g % 4]

            @blk.vector
            def _(eng, g=g, qout=qout, q8w=q8w, s2p=s2p):
                eng.wait_ge(s_gc, gc_wq[g])
                if g >= 2:
                    eng.wait_ge(s_sp, sp_wq[g - 2])
                eng.tensor_tensor(out=qout[:, :].rearrange("p (b k) -> p b k", k=32),
                                  in0=q8w[:, :].rearrange("p (b k) -> p b k", k=32),
                                  in1=s2p[:, :].rearrange("p (b o) -> p b o", o=1
                                      ).broadcast_to([128, 64, 32]),
                                  op=ALU.mult).then_inc(s_dv, 1)
                V.dv += 1
            dv_wq.append(V.dv)

            @blk.scalar
            def _(eng, g=g, qout=qout, dst=dst, rt=rt, co=co):
                eng.wait_ge(s_dv, dv_wq[g])
                eng.dma_start(out=dst[rt * 128:(rt + 1) * 128, co:co + C],
                              in_=qout[:, :]).then_inc(s_sp, 16)
                V.sp += 16
            sp_wq.append(V.sp)

        gp_head = []
        pe_headmm = []
        dv_headev = []
        ac_headsig = []

        def emit_head(h):
            wk_b, wv_b, wr_b = wbufs[h % 2]

            @blk.gpsimd
            def _(eng, h=h, wk_b=wk_b, wv_b=wv_b, wr_b=wr_b):
                if h == 0:
                    eng.wait_ge(s_sp, sp_xnxq)
                if h >= 1:
                    eng.wait_ge(s_pe, pe_headmm[h - 1])
                for W, buf in ((WkT, wk_b), (WvT, wv_b), (WrT, wr_b)):
                    eng.dma_start(
                        out=buf[:, :].rearrange("p (c n) -> p c n", c=NCT),
                        in_=W.rearrange("(c p) n -> p c n", p=128)[:, :, h * 128:(h + 1) * 128],
                    ).then_inc(s_gp, 16)
                    V.gp += 16
            gp_head.append(V.gp)

            @blk.tensor
            def _(eng, h=h, wk_b=wk_b, wv_b=wv_b, wr_b=wr_b):
                eng.wait_ge(s_gp, gp_head[h])
                eng.wait_ge(s_tr, tr_xn)
                if h >= 1:
                    eng.wait_ge(s_dv, dv_headev[h - 1])
                    eng.wait_ge(s_ac, ac_headsig[h - 1])
                mm = None
                for mi, buf in ((0, wk_b), (1, wv_b), (2, wr_b)):
                    for half in range(2):
                        bank = ps[mi * 2 + half]
                        for ci in range(NCT):
                            mm = eng.matmul(bank[:, :],
                                            buf[:, ci * 128:(ci + 1) * 128],
                                            xnT[:, ci * T_LOC + half * 512: ci * T_LOC + half * 512 + 512],
                                            start=(ci == 0), stop=(ci == NCT - 1))
                mm.then_inc(s_pe, 1)
                V.pe += 1
            pe_headmm.append(V.pe)

            @blk.vector
            def _(eng, h=h):
                eng.wait_ge(s_pe, pe_headmm[h])
                eng.wait_ge(s_dv, dv_fe)
                eng.wait_ge(s_ac, V.ac)
                eng.tensor_scalar(out=w_h[:, 0:512], in0=ps[0][:, :], scalar1=-10.0,
                                  scalar2=10.0, op0=ALU.max, op1=ALU.min)
                eng.tensor_scalar(out=w_h[:, 512:1024], in0=ps[1][:, :], scalar1=-10.0,
                                  scalar2=10.0, op0=ALU.max, op1=ALU.min).then_inc(s_dv, 1)
                V.dv += 1
            dv_clip = V.dv

            @blk.scalar
            def _(eng, h=h, dv_clip=dv_clip):
                eng.wait_ge(s_dv, dv_clip)
                eng.activation(out=w_h[:, :], in_=w_h[:, :], func=AF.Exp).then_inc(s_ac, 1)
                eng.activation(out=sigr[:, h * T_LOC: h * T_LOC + 512],
                               in_=ps[4][:, :], func=AF.Sigmoid)
                eng.activation(out=sigr[:, h * T_LOC + 512: h * T_LOC + 1024],
                               in_=ps[5][:, :], func=AF.Sigmoid).then_inc(s_ac, 1)
                V.ac += 2
            ac_headsig.append(V.ac)
            ac_w = V.ac - 1

            @blk.vector
            def _(eng, h=h, ac_w=ac_w):
                eng.wait_ge(s_ac, ac_w)
                eng.tensor_tensor(out=w_h[:, 0:1], in0=w_h[:, 0:1],
                                  in1=fe_all[:, h:h + 1], op=ALU.mult)
                eng.drain()
                eng.tensor_tensor(out=a_h[:, 0:512], in0=w_h[:, 0:512],
                                  in1=ps[2][:, :], op=ALU.mult)
                eng.tensor_tensor(out=a_h[:, 512:1024], in0=w_h[:, 512:1024],
                                  in1=ps[3][:, :], op=ALU.mult)
                eng.drain()
                dbc = d_all[:, h:h + 1].broadcast_to([128, T_LOC])
                eng.tensor_tensor_scan(out=scn[:, :], data0=dbc, data1=a_h[:, :],
                                       initial=0.0, op0=ALU.mult, op1=ALU.add)
                eng.tensor_tensor_scan(out=scd[:, :], data0=dbc, data1=w_h[:, :],
                                       initial=0.0, op0=ALU.mult, op1=ALU.add)
                eng.drain()
                eng.tensor_copy(out=numl[:, h * T_LOC:(h + 1) * T_LOC], in_=scn[:, :])
                eng.tensor_copy(out=denl[:, h * T_LOC:(h + 1) * T_LOC], in_=scd[:, :])
                eng.tensor_copy(out=Lo_n[:, h:h + 1], in_=scn[:, T_LOC - 1:T_LOC])
                eng.tensor_copy(out=Lo_d[:, h:h + 1], in_=scd[:, T_LOC - 1:T_LOC]).then_inc(s_dv, 1)
                V.dv += 1
            dv_headev.append(V.dv)

        for h in range(H):
            emit_head(h)
            for g in range(h * 12, min((h + 1) * 12, len(funits))):
                emit_funit_front(g)
                if g >= 1:
                    emit_funit_back(g - 1)
        emit_funit_back(len(funits) - 1)
        sp_wq_done = V.sp

        for nm, src, n_, w_ in (("gwq", sp_gw, I, C), ("dwq", sp_dw, C, I)):
            if nm in dbg:
                @blk.sync
                def _(eng, nm=nm, src=src):
                    eng.wait_ge(s_sp, sp_wq_done)
                    eng.dma_start(out=dbg[nm], in_=src).then_inc(s_out, 16)
                    V.outv += 16

        @blk.gpsimd
        def _(eng):
            eng.wait_ge(s_dv, dv_headev[H - 1])
            eng.dma_start(out=cb_in.ap()[:, 0:H], in_=Lo_n[:, :]).then_inc(s_gp, 16)
            eng.dma_start(out=cb_in.ap()[:, H:2 * H], in_=Lo_d[:, :]).then_inc(s_gp, 16)
            V.gp += 32
            eng.wait_ge(s_gp, V.gp)
            eng.collective_compute(
                "AllGather", ALU.bypass, replica_groups=[list(range(8))],
                ins=[cb_in.ap()], outs=[cb_out.ap()],
            ).then_inc(s_cc, 1)
            V.cc += 1
            eng.wait_ge(s_cc, V.cc)
            eng.dma_start(out=Lg[:, :].rearrange("p (r n) -> p r n", r=8),
                          in_=cb_out.ap().rearrange("r p n -> p r n")).then_inc(s_gp, 16)
            V.gp += 16
        gp_carry = V.gp

        @blk.vector
        def _(eng):
            eng.wait_ge(s_gp, gp_carry)
            eng.memset(car_n[:, :], 0.0)
            eng.memset(car_d[:, :], 0.0)
            eng.nop().then_inc(s_dv, 1)
            V.dv += 1
        for j in range(8):
            dv_prev = V.dv

            @blk.vector
            def _(eng, j=j, dv_prev=dv_prev):
                eng.wait_ge(s_dv, dv_prev)
                eng.wait_ge(s_ac, V.ac)
                eng.tensor_scalar(out=tt16[:, :], in0=e_all[:, :],
                                  scalar1=cws[:, j:j + 1], scalar2=None,
                                  op0=ALU.mult).then_inc(s_dv, 1)
                V.dv += 1
            dv_t = V.dv

            @blk.scalar
            def _(eng, dv_t=dv_t):
                eng.wait_ge(s_dv, dv_t)
                eng.activation(out=co16[:, :], in_=tt16[:, :], func=AF.Exp,
                               scale=-1.0).then_inc(s_ac, 1)
                V.ac += 1
            ac_c = V.ac

            @blk.vector
            def _(eng, j=j, ac_c=ac_c):
                eng.wait_ge(s_ac, ac_c)
                eng.tensor_tensor(out=tt16[:, :], in0=co16[:, :],
                                  in1=Lg[:, j * 2 * H: j * 2 * H + H], op=ALU.mult)
                eng.drain()
                eng.tensor_tensor(out=car_n[:, :], in0=car_n[:, :], in1=tt16[:, :],
                                  op=ALU.add)
                eng.drain()
                eng.tensor_tensor(out=tt16[:, :], in0=co16[:, :],
                                  in1=Lg[:, j * 2 * H + H: (j + 1) * 2 * H], op=ALU.mult)
                eng.drain()
                eng.tensor_tensor(out=car_d[:, :], in0=car_d[:, :], in1=tt16[:, :],
                                  op=ALU.add).then_inc(s_dv, 1)
                V.dv += 1
        dv_carry = V.dv

        for h in range(H):
            @blk.vector
            def _(eng, h=h):
                eng.wait_ge(s_dv, dv_carry)
                eng.wait_ge(s_ac, V.ac)
                dbc = d_all[:, h:h + 1].broadcast_to([128, T_LOC])
                zbc = zero1[:, 0:1].broadcast_to([128, T_LOC])
                eng.tensor_tensor_scan(out=scn[:, :], data0=dbc, data1=zbc,
                                       initial=1.0, op0=ALU.mult, op1=ALU.add)
                eng.drain()
                eng.scalar_tensor_tensor(out=scd[:, :], in0=scn[:, :],
                                         scalar=car_d[:, h:h + 1],
                                         in1=denl[:, h * T_LOC:(h + 1) * T_LOC],
                                         op0=ALU.mult, op1=ALU.add)
                eng.scalar_tensor_tensor(out=scn[:, :], in0=scn[:, :],
                                         scalar=car_n[:, h:h + 1],
                                         in1=numl[:, h * T_LOC:(h + 1) * T_LOC],
                                         op0=ALU.mult, op1=ALU.add).then_inc(s_dv, 1)
                V.dv += 1
            dv_nf = V.dv

            @blk.scalar
            def _(eng, dv_nf=dv_nf):
                eng.wait_ge(s_dv, dv_nf)
                eng.activation(out=scd[:, :], in_=scd[:, :], func=AF.Ln,
                               bias=ceps6[:, 0:1])
                eng.drain()
                eng.activation(out=scd[:, :], in_=scd[:, :], func=AF.Exp,
                               scale=-1.0).then_inc(s_ac, 1)
                V.ac += 1
            ac_rd = V.ac

            @blk.vector
            def _(eng, h=h, ac_rd=ac_rd):
                eng.wait_ge(s_ac, ac_rd)
                eng.tensor_tensor(out=scn[:, :], in0=scn[:, :], in1=scd[:, :],
                                  op=ALU.mult)
                eng.drain()
                eng.tensor_tensor(out=numl[:, h * T_LOC:(h + 1) * T_LOC],
                                  in0=scn[:, :],
                                  in1=sigr[:, h * T_LOC:(h + 1) * T_LOC],
                                  op=ALU.mult).then_inc(s_dv, 1)
                V.dv += 1
        dv_rwkv = V.dv

        @blk.scalar
        def _(eng):
            eng.wait_ge(s_dv, dv_rwkv)
            for ct in range(NCT):
                eng.dma_start(out=sp_rw[ct * 128:(ct + 1) * 128, :],
                              in_=numl[:, ct * T_LOC:(ct + 1) * T_LOC]).then_inc(s_sp, 16)
                V.sp += 16
        sp_rwd = V.sp

        if "rwkvT" in dbg:
            @blk.sync
            def _(eng):
                eng.wait_ge(s_dv, dv_rwkv)
                for ct in range(NCT):
                    eng.dma_start(out=dbg["rwkvT"][ct * 128:(ct + 1) * 128, :],
                                  in_=numl[:, ct * T_LOC:(ct + 1) * T_LOC]).then_inc(s_out, 16)
                    V.outv += 16

        free("rsF3", "rsF2", "rsF1", "rsF0", "s2F3", "s2F2", "s2F1", "s2F0", "bmF3", "bmF2", "bmF1", "bmF0",
             "q8F1", "q8F0", "qoutF1", "qoutF0", "natF3", "natF2", "natF1", "natF0",
             "a_h", "w_h",
             "wrt0", "wvt0", "wkt0", "xnT",
             "sigr", "denl", "numl", "scd", "scn")
        barrier()

        xqT = sb("xqT", [128, NCT * T_LOC], BF16)
        gwqA = sb("gwqA", [128, NCT * 512], BF16)
        uwqA = sb("uwqA", [128, NCT * 512], BF16)
        gwqB = sb("gwqB", [128, NCT * 512], BF16)
        uwqB = sb("uwqB", [128, NCT * 512], BF16)
        sg512 = sb("sg512", [128, 512], F32)
        h512 = sb("h512", [128, 512], F32)
        q8h = sb("q8h", [128, 512], FP8)
        hstage = sb("hstage", [128, NT * 512], BF16)

        @blk.sync
        def _(eng):
            for ct in range(NCT):
                eng.dma_start_transpose(
                    out=xqT[:, ct * T_LOC:(ct + 1) * T_LOC],
                    in_=sp_xq[:, ct * 128:(ct + 1) * 128]).then_inc(s_tr, 16)
                V.tr += 16
        tr_xq = V.tr

        NICH = I // 512
        tr_hch = []
        pe_chunk = []
        dv_hq = []
        sp_hch = []
        for ic in range(NICH):
            gq_b, uq_b = ((gwqA, uwqA), (gwqB, uwqB))[ic % 2]

            @blk.sync
            def _(eng, ic=ic, gq_b=gq_b, uq_b=uq_b):
                if ic >= 2:
                    eng.wait_ge(s_pe, pe_chunk[ic - 2])
                for ct in range(NCT):
                    eng.dma_start_transpose(
                        out=gq_b[:, ct * 512:(ct + 1) * 512],
                        in_=sp_gw[ic * 512:(ic + 1) * 512, ct * 128:(ct + 1) * 128],
                    ).then_inc(s_tr, 16)
                    eng.dma_start_transpose(
                        out=uq_b[:, ct * 512:(ct + 1) * 512],
                        in_=sp_uw[ic * 512:(ic + 1) * 512, ct * 128:(ct + 1) * 128],
                    ).then_inc(s_tr, 16)
                    V.tr += 32
            tr_hch.append(V.tr)

            for tt in range(NT):
                bk_g = ps[(tt % 2) * 2]
                bk_u = ps[(tt % 2) * 2 + 1]
                idx = ic * NT + tt

                @blk.tensor
                def _(eng, ic=ic, tt=tt, gq_b=gq_b, uq_b=uq_b, bk_g=bk_g,
                      bk_u=bk_u, idx=idx):
                    eng.wait_ge(s_tr, tr_hch[ic])
                    if idx >= 2:
                        eng.wait_ge(s_dv, dv_hq[idx - 2])
                    for ci in range(NCT):
                        eng.matmul(bk_g[:, :],
                                   xqT[:, ci * T_LOC + tt * 128: ci * T_LOC + tt * 128 + 128],
                                   gq_b[:, ci * 512:(ci + 1) * 512],
                                   start=(ci == 0), stop=(ci == NCT - 1))
                    mm = None
                    for ci in range(NCT):
                        mm = eng.matmul(bk_u[:, :],
                                        xqT[:, ci * T_LOC + tt * 128: ci * T_LOC + tt * 128 + 128],
                                        uq_b[:, ci * 512:(ci + 1) * 512],
                                        start=(ci == 0), stop=(ci == NCT - 1))
                    mm.then_inc(s_pe, 1)
                    V.pe += 1
                pe_gu = V.pe

                @blk.scalar
                def _(eng, pe_gu=pe_gu, bk_g=bk_g):
                    eng.wait_ge(s_pe, pe_gu)
                    eng.wait_ge(s_dv, V.dv)
                    eng.activation(out=sg512[:, :], in_=bk_g[:, :],
                                   func=AF.Silu).then_inc(s_ac, 1)
                    V.ac += 1
                ac_sg = V.ac

                @blk.vector
                def _(eng, tt=tt, ac_sg=ac_sg, bk_u=bk_u, ic=ic):
                    eng.wait_ge(s_ac, ac_sg)
                    if sp_hch and tt == 0:
                        eng.wait_ge(s_sp, sp_hch[-1])
                    eng.tensor_tensor(out=h512[:, :], in0=sg512[:, :], in1=bk_u[:, :],
                                      op=ALU.mult)
                    eng.drain()
                    eng.tensor_reduce(out=bm64[:, 0:16],
                                      in_=h512[:, :].rearrange("p (b k) -> p b k", k=32),
                                      axis=AX.X, op=ALU.max, apply_absolute_value=True)
                    eng.drain()
                    eng.tensor_scalar(out=s2x[:, 0:16], in0=bm64[:, 0:16], scalar1=R448_2,
                                      scalar2=2e-30, op0=ALU.mult, op1=ALU.max)
                    eng.drain()
                    eng.reciprocal(out=rs2[:, 0:16], in_=s2x[:, 0:16])
                    eng.drain()
                    eng.tensor_tensor(out=q8h[:, :].rearrange("p (b k) -> p b k", k=32),
                                      in0=h512[:, :].rearrange("p (b k) -> p b k", k=32),
                                      in1=rs2[:, 0:16].rearrange("p (b o) -> p b o", o=1
                                          ).broadcast_to([128, 16, 32]),
                                      op=ALU.mult)
                    eng.drain()
                    eng.tensor_tensor(out=hstage[:, tt * 512:(tt + 1) * 512
                                      ].rearrange("p (b k) -> p b k", k=32),
                                      in0=q8h[:, :].rearrange("p (b k) -> p b k", k=32),
                                      in1=s2x[:, 0:16].rearrange("p (b o) -> p b o", o=1
                                          ).broadcast_to([128, 16, 32]),
                                      op=ALU.mult).then_inc(s_dv, 1)
                    V.dv += 1
                dv_hq.append(V.dv)
            pe_chunk.append(V.pe)

            @blk.scalar
            def _(eng, ic=ic):
                eng.wait_ge(s_dv, dv_hq[-1])
                eng.dma_start(
                    out=sp_hq.rearrange("(t p) i -> p t i", p=128)[:, :, ic * 512:(ic + 1) * 512],
                    in_=hstage[:, :].rearrange("p (t n) -> p t n", t=NT),
                ).then_inc(s_sp, 16)
                V.sp += 16
            sp_hch.append(V.sp)
        sp_hqd = V.sp

        if "hq" in dbg:
            @blk.sync
            def _(eng):
                eng.wait_ge(s_sp, sp_hqd)
                eng.dma_start(out=dbg["hq"], in_=sp_hq).then_inc(s_out, 16)
                V.outv += 16

        free("hstage", "q8h", "h512", "sg512", "uwqB", "gwqB", "uwqA", "gwqA",
             "xqT")
        barrier()

        acc = sb("acc", [128, NT * C], F32)
        hqgA = sb("hqgA", [128, 8 * T_LOC], BF16)
        hqgB = sb("hqgB", [128, 8 * T_LOC], BF16)
        dwqA = sb("dwqA", [128, 8 * C], BF16)
        dwqB = sb("dwqB", [128, 8 * C], BF16)

        @blk.vector
        def _(eng):
            eng.memset(acc[:, :], 0.0)
            eng.nop().then_inc(s_dv, 1)
            V.dv += 1
        dv_accz = V.dv

        NG8 = 8
        tr_g8 = []
        pe_g8 = []
        dv_unit = []
        pe_unit = []
        for g8 in range(NG8):
            hq_b = (hqgA, hqgB)[g8 % 2]
            dq_b = (dwqA, dwqB)[g8 % 2]

            @blk.sync
            def _(eng, g8=g8, hq_b=hq_b, dq_b=dq_b):
                if g8 >= 2:
                    eng.wait_ge(s_pe, pe_g8[g8 - 2])
                for k in range(8):
                    it = g8 * 8 + k
                    eng.dma_start_transpose(
                        out=hq_b[:, k * T_LOC:(k + 1) * T_LOC],
                        in_=sp_hq[:, it * 128:(it + 1) * 128]).then_inc(s_tr, 16)
                    eng.dma_start_transpose(
                        out=dq_b[:, k * C:(k + 1) * C],
                        in_=sp_dw[:, it * 128:(it + 1) * 128]).then_inc(s_tr, 16)
                    V.tr += 32
            tr_g8.append(V.tr)

            for u in range(8):
                ncn = u // 2
                qd = u % 2
                half = (u % 2)
                gu = g8 * 8 + u

                @blk.tensor
                def _(eng, g8=g8, u=u, ncn=ncn, qd=qd, half=half, gu=gu,
                      hq_b=hq_b, dq_b=dq_b):
                    eng.wait_ge(s_tr, tr_g8[g8])
                    if gu == 0:
                        eng.wait_ge(s_dv, dv_accz)
                    if gu >= 2:
                        eng.wait_ge(s_dv, dv_unit[gu - 2])
                    mm = None
                    for tti in range(4):
                        tt = qd * 4 + tti
                        for k in range(8):
                            mm = eng.matmul(ps[half * 4 + tti][:, :],
                                            hq_b[:, k * T_LOC + tt * 128: k * T_LOC + tt * 128 + 128],
                                            dq_b[:, k * C + ncn * 512: k * C + ncn * 512 + 512],
                                            start=(k == 0), stop=(k == 7))
                    mm.then_inc(s_pe, 1)
                    V.pe += 1
                pe_unit.append(V.pe)

                @blk.vector
                def _(eng, ncn=ncn, qd=qd, half=half, pe_u=V.pe):
                    eng.wait_ge(s_pe, pe_u)
                    for tti in range(4):
                        tt = qd * 4 + tti
                        sl = slice(tt * C + ncn * 512, tt * C + ncn * 512 + 512)
                        eng.tensor_tensor(out=acc[:, sl], in0=acc[:, sl],
                                          in1=ps[half * 4 + tti][:, :], op=ALU.add)
                    eng.nop().then_inc(s_dv, 1)
                    V.dv += 1
                dv_unit.append(V.dv)
            pe_g8.append(V.pe)
        dv_down = V.dv

        free("dwqB", "dwqA", "hqgB", "hqgA")
        barrier()

        rwT = sb("rwT", [128, NCT * T_LOC], BF16)
        wotc = sb("wotc", [128, NCT * 512], BF16)
        xbufJ = sb("xbufJ", [128, NT * C], F32)
        gsil = sb("gsil", [128, C], F32)
        otile = sb("otile", [128, 2 * 512], F32)

        @blk.scalar
        def _(eng):
            eng.wait_ge(s_sp, sp_rwd)
            eng.dma_start(out=rwT[:, :].rearrange("p (c n) -> p c n", c=NCT),
                          in_=sp_rw.rearrange("(c p) n -> p c n", p=128)).then_inc(s_ld, 16)
            eng.dma_start(out=xbufJ[:, :].rearrange("p (t c) -> p t c", t=NT),
                          in_=x_tm.rearrange("(t p) c -> p t c", p=128)).then_inc(s_ld, 16)
            V.ld += 32
        ld_rw = V.ld

        gp_wot = []
        pe_s = []
        dv_fin = []
        for ncn in range(4):
            @blk.gpsimd
            def _(eng, ncn=ncn):
                if ncn >= 1:
                    eng.wait_ge(s_pe, pe_s[-1])
                eng.dma_start(out=wotc[:, :].rearrange("p (c n) -> p c n", c=NCT),
                              in_=WoT.rearrange("(c p) n -> p c n", p=128)[:, :, ncn * 512:(ncn + 1) * 512],
                              ).then_inc(s_gp, 16)
                V.gp += 16
            gp_wot.append(V.gp)

            for tt in range(NT):
                bank = ps[tt % 4]
                idx = ncn * NT + tt

                @blk.tensor
                def _(eng, ncn=ncn, tt=tt, bank=bank, idx=idx):
                    eng.wait_ge(s_gp, gp_wot[ncn])
                    eng.wait_ge(s_ld, ld_rw)
                    if idx >= 4:
                        eng.wait_ge(s_dv, dv_fin[idx - 4])
                    mm = None
                    for ci in range(NCT):
                        mm = eng.matmul(bank[:, :],
                                        rwT[:, ci * T_LOC + tt * 128: ci * T_LOC + tt * 128 + 128],
                                        wotc[:, ci * 512:(ci + 1) * 512],
                                        start=(ci == 0), stop=(ci == NCT - 1))
                    mm.then_inc(s_pe, 1)
                    V.pe += 1
                pe_s.append(V.pe)
                pe_this = V.pe

                @blk.scalar
                def _(eng, ncn=ncn, tt=tt, pe_this=pe_this):
                    eng.wait_ge(s_pe, pe_this)
                    eng.wait_ge(s_dv, V.dv)
                    eng.activation(out=gsil[:, ncn * 512:(ncn + 1) * 512],
                                   in_=acc[:, tt * C + ncn * 512: tt * C + ncn * 512 + 512],
                                   func=AF.Silu).then_inc(s_ac, 1)
                    V.ac += 1
                ac_g = V.ac

                slot = slice((idx % 2) * 512, (idx % 2) * 512 + 512)
                out_done = V.outv

                @blk.vector
                def _(eng, ncn=ncn, tt=tt, ac_g=ac_g, bank=bank, slot=slot,
                      out_done=out_done, idx=idx):
                    eng.wait_ge(s_ac, ac_g)
                    if idx >= 2:
                        eng.wait_ge(s_out, out_done - 16)
                    gl = slice(ncn * 512, (ncn + 1) * 512)
                    eng.tensor_tensor(out=otile[:, slot], in0=bank[:, :],
                                      in1=gsil[:, gl], op=ALU.mult)
                    eng.drain()
                    eng.tensor_tensor(out=otile[:, slot], in0=otile[:, slot],
                                      in1=xbufJ[:, tt * C + ncn * 512: tt * C + ncn * 512 + 512],
                                      op=ALU.add).then_inc(s_dv, 1)
                    V.dv += 1
                dv_fin.append(V.dv)

                @blk.scalar
                def _(eng, ncn=ncn, tt=tt, slot=slot):
                    eng.wait_ge(s_dv, dv_fin[-1])
                    eng.dma_start(out=out[tt * 128:(tt + 1) * 128, ncn * 512:(ncn + 1) * 512],
                                  in_=otile[:, slot]).then_inc(s_out, 16)
                    V.outv += 16

        @blk.scalar
        def _(eng):
            eng.wait_ge(s_out, V.outv)

    for n in reversed(list(live)):
        live[n].__exit__(None, None, None)
    for cm in reversed(ctx):
        cm.__exit__(None, None, None)
    return nc


def make_inmaps(inputs):
    x = np.asarray(inputs["L0_attn_x"], dtype=np.float32).reshape(8192, 2048)
    WkT = np.ascontiguousarray(np.asarray(inputs["L0_attn_Wk"]).T)
    WvT = np.ascontiguousarray(np.asarray(inputs["L0_attn_Wv"]).T)
    WrT = np.ascontiguousarray(np.asarray(inputs["L0_attn_Wr"]).T)
    WoT = np.ascontiguousarray(np.asarray(inputs["L0_attn_Wo"]).T)
    gw = np.ascontiguousarray(np.asarray(inputs["L0_ffn_gate_w"], dtype=np.float32))
    uw = np.ascontiguousarray(np.asarray(inputs["L0_ffn_up_w"], dtype=np.float32))
    dw = np.ascontiguousarray(np.asarray(inputs["L0_ffn_down_w"], dtype=np.float32))
    tdT = np.ascontiguousarray(np.asarray(inputs["L0_attn_time_decay"]).T.astype(np.float32))
    tfT = np.ascontiguousarray(np.asarray(inputs["L0_attn_time_first"]).T.astype(np.float32))
    maps = []
    for i in range(8):
        cw_row = np.zeros(8, np.float32)
        for j in range(8):
            cw_row[j] = 1024.0 * (i - 1 - j) if j < i else 1e9
        maps.append({
            "x_tm": np.ascontiguousarray(x[i * 1024:(i + 1) * 1024]),
            "WkT": WkT, "WvT": WvT, "WrT": WrT, "WoT": WoT,
            "gw": gw, "uw": uw, "dw": dw,
            "tdT": tdT, "tfT": tfT,
            "cwi": np.tile(cw_row, (128, 1)),
            "c0f": np.full((128, 1), 1.0 if i == 0 else 0.0, np.float32),
        })
    return maps


def kernel(**inputs):
    nc = build_nc()
    maps = make_inmaps(inputs)
    res = run_bass_kernel_spmd(nc, maps, core_ids=list(range(8)))
    o = np.concatenate([np.asarray(r["out"]) for r in res.results], axis=0)
    return o.reshape(1, 8192, 2048).astype(np.float32)


# version 07
# speedup vs baseline: 1.4085x; 1.1455x over previous
import numpy as np
import concourse.bass as bass
import concourse.mybir as mybir
from concourse.bass_utils import run_bass_kernel_spmd

F32 = mybir.dt.float32
BF16 = mybir.dt.bfloat16
FP8 = mybir.dt.float8e4
ALU = mybir.AluOpType
AF = mybir.ActivationFunctionType
AX = mybir.AxisListType

T_LOC = 1024
C = 2048
H = 16
S = 128
I = 8192
NT = T_LOC // 128
NCT = C // 128
NIT = I // 128

R448 = float(np.float32(1.0) / np.float32(448.0))
R448_2 = float(np.float32(2.0) * (np.float32(1.0) / np.float32(448.0)))


def build_nc(debug=()):
    nc = bass.Bass("TRN2", target_bir_lowering=False, debug=False,
                   detect_race_conditions=False)
    x_tm = nc.dram_tensor("x_tm", [T_LOC, C], F32, kind="ExternalInput").ap()
    WkT = nc.dram_tensor("WkT", [C, C], F32, kind="ExternalInput").ap()
    WvT = nc.dram_tensor("WvT", [C, C], F32, kind="ExternalInput").ap()
    WrT = nc.dram_tensor("WrT", [C, C], F32, kind="ExternalInput").ap()
    WoT = nc.dram_tensor("WoT", [C, C], F32, kind="ExternalInput").ap()
    gw = nc.dram_tensor("gw", [I, C], F32, kind="ExternalInput").ap()
    uw = nc.dram_tensor("uw", [I, C], F32, kind="ExternalInput").ap()
    dw = nc.dram_tensor("dw", [C, I], F32, kind="ExternalInput").ap()
    tdT = nc.dram_tensor("tdT", [S, H], F32, kind="ExternalInput").ap()
    tfT = nc.dram_tensor("tfT", [S, H], F32, kind="ExternalInput").ap()
    cwi = nc.dram_tensor("cwi", [S, 8], F32, kind="ExternalInput").ap()
    c0f = nc.dram_tensor("c0f", [S, 1], F32, kind="ExternalInput").ap()
    out = nc.dram_tensor("out", [T_LOC, C], F32, kind="ExternalOutput").ap()
    sp_xn = nc.dram_tensor("sp_xn", [T_LOC, C], BF16).ap()
    sp_xq = nc.dram_tensor("sp_xq", [T_LOC, C], BF16).ap()
    sp_gw = nc.dram_tensor("sp_gw", [I, C], BF16).ap()
    sp_uw = nc.dram_tensor("sp_uw", [I, C], BF16).ap()
    sp_dw = nc.dram_tensor("sp_dw", [C, I], BF16).ap()
    sp_hq = nc.dram_tensor("sp_hq", [T_LOC, I], BF16).ap()
    sp_rw = nc.dram_tensor("sp_rw", [C, T_LOC], BF16).ap()
    cb_in = nc.dram_tensor("cb_in", [S, 2 * H], F32)
    cb_out = nc.dram_tensor("cb_out", [8, S, 2 * H], F32)

    dbg = {}
    for name, shape, dt in (
        ("rwkvT", [C, T_LOC], BF16), ("gwq", [I, C], BF16),
        ("dwq", [C, I], BF16), ("hq", [T_LOC, I], BF16),
    ):
        if name in debug:
            dbg[name] = nc.dram_tensor("dbg_" + name, shape, dt,
                                       kind="ExternalOutput").ap()

    live = {}

    def sb(name, shape, dt):
        cm = nc.sbuf_tensor(name, shape, dt)
        t = cm.__enter__()
        live[name] = cm
        return t

    def free(*names):
        for n in names:
            live.pop(n).__exit__(None, None, None)

    ctx = []

    def ps_(name):
        cm = nc.psum_tensor(name, [128, 512], F32)
        t = cm.__enter__()
        ctx.append(cm)
        return t

    def semi(name):
        cm = nc.semaphore(name)
        s = cm.__enter__()
        ctx.append(cm)
        return s

    stat = sb("stat", [128, 16], F32)
    bm64 = sb("bm64", [128, 16], F32)
    s2x = sb("s2x", [128, 16], F32)
    rs2 = sb("rs2", [128, 16], F32)
    ceps5 = sb("ceps5", [128, 1], F32)
    ceps6 = sb("ceps6", [128, 1], F32)
    zero1 = sb("zero1", [128, 1], F32)
    tds = sb("tds", [128, H], F32)
    tfs = sb("tfs", [128, H], F32)
    cws = sb("cws", [128, 8], F32)
    c0s = sb("c0s", [128, 1], F32)
    e_all = sb("e_all", [128, H], F32)
    d_all = sb("d_all", [128, H], F32)
    fe_all = sb("fe_all", [128, H], F32)
    Lo_n = sb("Lo_n", [128, H], F32)
    Lo_d = sb("Lo_d", [128, H], F32)
    Lg = sb("Lg", [128, 8 * 2 * H], F32)
    car_n = sb("car_n", [128, H], F32)
    car_d = sb("car_d", [128, H], F32)
    co16 = sb("co16", [128, H], F32)
    tt16 = sb("tt16", [128, H], F32)

    ps = [ps_("psb%d" % i) for i in range(8)]

    s_ld = semi("s_ld")
    s_gp = semi("s_gp")
    s_tr = semi("s_tr")
    s_sp = semi("s_sp")
    s_pe = semi("s_pe")
    s_dv = semi("s_dv")
    s_ac = semi("s_ac")
    s_gc = semi("s_gc")
    s_cc = semi("s_cc")
    s_out = semi("s_out")

    class V:
        ld = gp = tr = sp = pe = dv = ac = gc = cc = outv = 0

    with nc.Block() as blk:

        def barrier():
            pairs = [(s_ld, V.ld), (s_gp, V.gp), (s_tr, V.tr), (s_sp, V.sp),
                     (s_pe, V.pe), (s_dv, V.dv), (s_ac, V.ac), (s_gc, V.gc)]
            for deco in (blk.sync, blk.scalar, blk.vector, blk.gpsimd, blk.tensor):
                @deco
                def _(eng, pairs=pairs):
                    for sem, val in pairs:
                        if val:
                            eng.wait_ge(sem, val)

        scn = sb("scn", [128, T_LOC], F32)
        scd = sb("scd", [128, T_LOC], F32)
        numl = sb("numl", [128, H * T_LOC], BF16)
        denl = sb("denl", [128, H * T_LOC], BF16)
        sigr = sb("sigr", [128, H * T_LOC], BF16)

        xbuf = sb("xbuf", [128, NT * C], F32)
        xn32 = sb("xn32", [128, C], F32)
        xc32 = sb("xc32", [128, C], F32)
        sq32 = sb("sq32", [128, C], F32)
        xnbf = sb("xnbf", [128, C], BF16)
        xqbf = sb("xqbf", [128, C], BF16)
        q8a = sb("q8a", [128, C], FP8)
        bmA = sb("bmA", [128, 64], F32)
        s2A = sb("s2A", [128, 64], F32)
        rsA = sb("rsA", [128, 64], F32)

        @blk.scalar
        def _(eng):
            eng.dma_start(out=xbuf[:, :].rearrange("p (t c) -> p t c", t=NT),
                          in_=x_tm.rearrange("(t p) c -> p t c", p=128)).then_inc(s_ld, 16)
            V.ld += 16
        ld_x = V.ld

        @blk.gpsimd
        def _(eng):
            eng.dma_start(out=tds[:, :], in_=tdT).then_inc(s_gp, 16)
            eng.dma_start(out=tfs[:, :], in_=tfT).then_inc(s_gp, 16)
            eng.dma_start(out=cws[:, :], in_=cwi).then_inc(s_gp, 16)
            eng.dma_start(out=c0s[:, :], in_=c0f).then_inc(s_gp, 16)
            V.gp += 64
        gp_const = V.gp

        @blk.vector
        def _(eng):
            eng.memset(ceps5[:, :], 1e-5)
            eng.memset(ceps6[:, :], 1e-6)
            eng.memset(zero1[:, :], 0.0)
            eng.nop().then_inc(s_dv, 1)
            V.dv += 1

        for tt in range(NT):
            xt = xbuf[:, tt * C:(tt + 1) * C]

            @blk.vector
            def _(eng, xt=xt):
                eng.wait_ge(s_ld, ld_x)
                eng.wait_ge(s_ac, V.ac)
                eng.wait_ge(s_sp, V.sp)
                eng.tensor_reduce(out=stat[:, 0:1], in_=xt, axis=AX.X, op=ALU.add,
                                  negate=True)
                eng.drain()
                eng.tensor_scalar(out=stat[:, 1:2], in0=stat[:, 0:1],
                                  scalar1=float(1.0 / C), scalar2=None, op0=ALU.mult)
                eng.drain()
                eng.tensor_scalar(out=xc32[:, :], in0=xt, scalar1=stat[:, 1:2],
                                  scalar2=None, op0=ALU.add)
                eng.drain()
                eng.scalar_tensor_tensor(out=sq32[:, :], in0=xc32[:, :], scalar=1.0,
                                         in1=xc32[:, :], op0=ALU.mult, op1=ALU.mult,
                                         accum_out=stat[:, 2:3]).then_inc(s_dv, 1)
                V.dv += 1
            dv_xc = V.dv

            @blk.scalar
            def _(eng, dv_xc=dv_xc):
                eng.wait_ge(s_dv, dv_xc)
                eng.activation(out=stat[:, 3:4], in_=stat[:, 2:3], func=AF.Sqrt,
                               scale=float(1.0 / C), bias=ceps5[:, 0:1]).then_inc(s_ac, 1)
                V.ac += 1
            ac_std = V.ac

            @blk.vector
            def _(eng, ac_std=ac_std):
                eng.wait_ge(s_ac, ac_std)
                eng.reciprocal(out=stat[:, 4:5], in_=stat[:, 3:4])
                eng.drain()
                eng.tensor_scalar(out=xn32[:, :], in0=xc32[:, :], scalar1=stat[:, 4:5],
                                  scalar2=None, op0=ALU.mult)
                eng.drain()
                eng.tensor_copy(out=xnbf[:, :], in_=xn32[:, :])
                eng.tensor_reduce(out=bmA[:, :],
                                  in_=xn32[:, :].rearrange("p (b k) -> p b k", k=32),
                                  axis=AX.X, op=ALU.max, apply_absolute_value=True)
                eng.drain()
                eng.tensor_scalar(out=s2A[:, :], in0=bmA[:, :], scalar1=R448_2,
                                  scalar2=2e-30, op0=ALU.mult, op1=ALU.max)
                eng.drain()
                eng.reciprocal(out=rsA[:, :], in_=s2A[:, :])
                eng.drain()
                eng.tensor_tensor(out=q8a[:, :].rearrange("p (b k) -> p b k", k=32),
                                  in0=xn32[:, :].rearrange("p (b k) -> p b k", k=32),
                                  in1=rsA[:, :].rearrange("p (b o) -> p b o", o=1
                                      ).broadcast_to([128, 64, 32]),
                                  op=ALU.mult)
                eng.drain()
                eng.tensor_tensor(out=xqbf[:, :].rearrange("p (b k) -> p b k", k=32),
                                  in0=q8a[:, :].rearrange("p (b k) -> p b k", k=32),
                                  in1=s2A[:, :].rearrange("p (b o) -> p b o", o=1
                                      ).broadcast_to([128, 64, 32]),
                                  op=ALU.mult).then_inc(s_dv, 1)
                V.dv += 1
            dv_xq = V.dv

            @blk.scalar
            def _(eng, tt=tt, dv_xq=dv_xq):
                eng.wait_ge(s_dv, dv_xq)
                eng.dma_start(out=sp_xn[tt * 128:(tt + 1) * 128, :],
                              in_=xnbf[:, :]).then_inc(s_sp, 16)
                eng.dma_start(out=sp_xq[tt * 128:(tt + 1) * 128, :],
                              in_=xqbf[:, :]).then_inc(s_sp, 16)
                V.sp += 32
        sp_xnxq = V.sp
        free("rsA", "s2A", "bmA", "q8a", "xqbf", "xnbf", "sq32", "xc32", "xn32", "xbuf")

        xnT = sb("xnT", [128, NCT * T_LOC], BF16)
        wbufs = []
        for i in range(2):
            wbufs.append((sb("wkt%d" % i, [128, NCT * 128], BF16),
                          sb("wvt%d" % i, [128, NCT * 128], BF16),
                          sb("wrt%d" % i, [128, NCT * 128], BF16)))
        w_h = sb("w_h", [128, T_LOC], F32)
        a_h = sb("a_h", [128, T_LOC], F32)
        natF = [sb("natF%d" % i, [128, C], F32) for i in range(3)]
        qoutF = [sb("qoutF%d" % i, [128, C], BF16) for i in range(2)]
        q8F = [sb("q8F%d" % i, [128, C], FP8) for i in range(2)]
        bmF = [sb("bmF%d" % i, [128, 64], F32) for i in range(2)]
        s2F = [sb("s2F%d" % i, [128, 64], F32) for i in range(2)]
        rsF = [sb("rsF%d" % i, [128, 64], F32) for i in range(2)]

        @blk.sync
        def _(eng):
            eng.wait_ge(s_sp, sp_xnxq)
            for ct in range(NCT):
                eng.dma_start_transpose(
                    out=xnT[:, ct * T_LOC:(ct + 1) * T_LOC],
                    in_=sp_xn[:, ct * 128:(ct + 1) * 128]).then_inc(s_tr, 16)
                V.tr += 16
        tr_xn = V.tr

        @blk.scalar
        def _(eng):
            eng.wait_ge(s_gp, gp_const)
            eng.activation(out=e_all[:, :], in_=tds[:, :], func=AF.Exp)
            eng.drain()
            eng.activation(out=d_all[:, :], in_=e_all[:, :], func=AF.Exp, scale=-1.0)
            eng.activation(out=fe_all[:, :], in_=tfs[:, :], func=AF.Exp).then_inc(s_ac, 1)
            V.ac += 1
        ac_decay = V.ac

        @blk.vector
        def _(eng):
            eng.wait_ge(s_ac, ac_decay)
            eng.scalar_tensor_tensor(out=fe_all[:, :], in0=fe_all[:, :], scalar=-1.0,
                                     in1=c0s[:, 0:1].broadcast_to([128, H]),
                                     op0=ALU.add, op1=ALU.mult)
            eng.drain()
            eng.tensor_scalar(out=fe_all[:, :], in0=fe_all[:, :], scalar1=1.0,
                              scalar2=None, op0=ALU.add).then_inc(s_dv, 1)
            V.dv += 1
        dv_fe = V.dv

        funits = []
        for it in range(NIT):
            funits.append((gw, sp_gw, it, 0))
        for it in range(NIT):
            funits.append((uw, sp_uw, it, 0))
        for ct in range(NCT):
            for cs in range(4):
                funits.append((dw, sp_dw, ct, cs * C))

        gc_wq = []
        dv_sc_l = []
        dv_wq = []
        sp_wq = []

        def emit_funit_front(g):
            src, dst, rt, co = funits[g]
            nat = natF[g % 3]
            bmp, s2p, rsp = bmF[g % 2], s2F[g % 2], rsF[g % 2]

            @blk.sync
            def _(eng, g=g, nat=nat, src=src, rt=rt, co=co):
                if g >= 3:
                    eng.wait_ge(s_gc, gc_wq[g - 3])
                eng.dma_start(out=nat[:, :],
                              in_=src[rt * 128:(rt + 1) * 128, co:co + C]).then_inc(s_ld, 16)
                V.ld += 16
            ld_nat = V.ld

            @blk.vector
            def _(eng, g=g, ld_nat=ld_nat, nat=nat, bmp=bmp, s2p=s2p, rsp=rsp):
                eng.wait_ge(s_ld, ld_nat)
                if g >= 2:
                    eng.wait_ge(s_gc, gc_wq[g - 2])
                eng.tensor_reduce(out=bmp[:, :],
                                  in_=nat[:, :].rearrange("p (b k) -> p b k", k=32),
                                  axis=AX.X, op=ALU.max, apply_absolute_value=True)
                eng.drain()
                eng.tensor_scalar(out=s2p[:, :], in0=bmp[:, :], scalar1=R448_2,
                                  scalar2=2e-30, op0=ALU.mult, op1=ALU.max)
                eng.drain()
                eng.reciprocal(out=rsp[:, :], in_=s2p[:, :]).then_inc(s_dv, 1)
                V.dv += 1
            dv_sc_l.append(V.dv)

            @blk.gpsimd
            def _(eng, g=g, nat=nat, rsp=rsp):
                q8w = q8F[g % 2]
                eng.wait_ge(s_dv, dv_sc_l[g])
                if g >= 2:
                    eng.wait_ge(s_dv, dv_wq[g - 2])
                eng.tensor_tensor(out=q8w[:, :].rearrange("p (b k) -> p b k", k=32),
                                  in0=nat[:, :].rearrange("p (b k) -> p b k", k=32),
                                  in1=rsp[:, :].rearrange("p (b o) -> p b o", o=1
                                      ).broadcast_to([128, 64, 32]),
                                  op=ALU.mult).then_inc(s_gc, 1)
                V.gc += 1
            gc_wq.append(V.gc)

        def emit_funit_back(g):
            src, dst, rt, co = funits[g]
            qout, q8w, s2p = qoutF[g % 2], q8F[g % 2], s2F[g % 2]

            @blk.vector
            def _(eng, g=g, qout=qout, q8w=q8w, s2p=s2p):
                eng.wait_ge(s_gc, gc_wq[g])
                if g >= 2:
                    eng.wait_ge(s_sp, sp_wq[g - 2])
                eng.tensor_tensor(out=qout[:, :].rearrange("p (b k) -> p b k", k=32),
                                  in0=q8w[:, :].rearrange("p (b k) -> p b k", k=32),
                                  in1=s2p[:, :].rearrange("p (b o) -> p b o", o=1
                                      ).broadcast_to([128, 64, 32]),
                                  op=ALU.mult).then_inc(s_dv, 1)
                V.dv += 1
            dv_wq.append(V.dv)

            @blk.scalar
            def _(eng, g=g, qout=qout, dst=dst, rt=rt, co=co):
                eng.wait_ge(s_dv, dv_wq[g])
                eng.dma_start(out=dst[rt * 128:(rt + 1) * 128, co:co + C],
                              in_=qout[:, :]).then_inc(s_sp, 16)
                V.sp += 16
            sp_wq.append(V.sp)

        gp_head = []
        pe_headmm = []
        dv_headev = []
        ac_headsig = []

        def emit_head_load(h):
            wk_b, wv_b, wr_b = wbufs[h % 2]

            @blk.gpsimd
            def _(eng, h=h, wk_b=wk_b, wv_b=wv_b, wr_b=wr_b):
                if h == 0:
                    eng.wait_ge(s_sp, sp_xnxq)
                if h >= 2:
                    eng.wait_ge(s_pe, pe_headmm[h - 2])
                for W, buf in ((WkT, wk_b), (WvT, wv_b), (WrT, wr_b)):
                    eng.dma_start(
                        out=buf[:, :].rearrange("p (c n) -> p c n", c=NCT),
                        in_=W.rearrange("(c p) n -> p c n", p=128)[:, :, h * 128:(h + 1) * 128],
                    ).then_inc(s_gp, 16)
                    V.gp += 16
            gp_head.append(V.gp)

        def emit_head_mm(h):
            wk_b, wv_b, wr_b = wbufs[h % 2]

            @blk.tensor
            def _(eng, h=h, wk_b=wk_b, wv_b=wv_b, wr_b=wr_b):
                eng.wait_ge(s_gp, gp_head[h])
                eng.wait_ge(s_tr, tr_xn)
                if h >= 1:
                    eng.wait_ge(s_dv, dv_headev[h - 1])
                    eng.wait_ge(s_ac, ac_headsig[h - 1])
                mm = None
                for mi, buf in ((0, wk_b), (1, wv_b), (2, wr_b)):
                    for half in range(2):
                        bank = ps[mi * 2 + half]
                        for ci in range(NCT):
                            mm = eng.matmul(bank[:, :],
                                            buf[:, ci * 128:(ci + 1) * 128],
                                            xnT[:, ci * T_LOC + half * 512: ci * T_LOC + half * 512 + 512],
                                            start=(ci == 0), stop=(ci == NCT - 1))
                mm.then_inc(s_pe, 1)
                V.pe += 1
            pe_headmm.append(V.pe)

        def emit_head_post(h):
            @blk.vector
            def _(eng, h=h):
                eng.wait_ge(s_pe, pe_headmm[h])
                eng.wait_ge(s_dv, dv_fe)
                eng.wait_ge(s_ac, V.ac)
                eng.tensor_scalar(out=w_h[:, 0:512], in0=ps[0][:, :], scalar1=-10.0,
                                  scalar2=10.0, op0=ALU.max, op1=ALU.min)
                eng.tensor_scalar(out=w_h[:, 512:1024], in0=ps[1][:, :], scalar1=-10.0,
                                  scalar2=10.0, op0=ALU.max, op1=ALU.min).then_inc(s_dv, 1)
                V.dv += 1
            dv_clip = V.dv

            @blk.scalar
            def _(eng, h=h, dv_clip=dv_clip):
                eng.wait_ge(s_dv, dv_clip)
                eng.activation(out=w_h[:, :], in_=w_h[:, :], func=AF.Exp).then_inc(s_ac, 1)
                eng.activation(out=sigr[:, h * T_LOC: h * T_LOC + 512],
                               in_=ps[4][:, :], func=AF.Sigmoid)
                eng.activation(out=sigr[:, h * T_LOC + 512: h * T_LOC + 1024],
                               in_=ps[5][:, :], func=AF.Sigmoid).then_inc(s_ac, 1)
                V.ac += 2
            ac_headsig.append(V.ac)
            ac_w = V.ac - 1

            @blk.vector
            def _(eng, h=h, ac_w=ac_w):
                eng.wait_ge(s_ac, ac_w)
                eng.tensor_tensor(out=w_h[:, 0:1], in0=w_h[:, 0:1],
                                  in1=fe_all[:, h:h + 1], op=ALU.mult)
                eng.drain()
                eng.tensor_tensor(out=a_h[:, 0:512], in0=w_h[:, 0:512],
                                  in1=ps[2][:, :], op=ALU.mult)
                eng.tensor_tensor(out=a_h[:, 512:1024], in0=w_h[:, 512:1024],
                                  in1=ps[3][:, :], op=ALU.mult)
                eng.drain()
                dbc = d_all[:, h:h + 1].broadcast_to([128, T_LOC])
                eng.tensor_tensor_scan(out=scn[:, :], data0=dbc, data1=a_h[:, :],
                                       initial=0.0, op0=ALU.mult, op1=ALU.add)
                eng.tensor_tensor_scan(out=scd[:, :], data0=dbc, data1=w_h[:, :],
                                       initial=0.0, op0=ALU.mult, op1=ALU.add)
                eng.drain()
                eng.tensor_copy(out=numl[:, h * T_LOC:(h + 1) * T_LOC], in_=scn[:, :])
                eng.tensor_copy(out=denl[:, h * T_LOC:(h + 1) * T_LOC], in_=scd[:, :])
                eng.tensor_copy(out=Lo_n[:, h:h + 1], in_=scn[:, T_LOC - 1:T_LOC])
                eng.tensor_copy(out=Lo_d[:, h:h + 1], in_=scd[:, T_LOC - 1:T_LOC]).then_inc(s_dv, 1)
                V.dv += 1
            dv_headev.append(V.dv)

        emit_head_load(0)
        emit_head_load(1)
        for h in range(H):
            emit_head_mm(h)
            if h + 2 < H:
                emit_head_load(h + 2)
            for g in range(h * 12, h * 12 + 6):
                emit_funit_front(g)
                if g >= 1:
                    emit_funit_back(g - 1)
            emit_head_post(h)
            for g in range(h * 12 + 6, (h + 1) * 12):
                emit_funit_front(g)
                if g >= 1:
                    emit_funit_back(g - 1)
        emit_funit_back(len(funits) - 1)
        sp_wq_done = V.sp

        for nm, src, n_, w_ in (("gwq", sp_gw, I, C), ("dwq", sp_dw, C, I)):
            if nm in dbg:
                @blk.sync
                def _(eng, nm=nm, src=src):
                    eng.wait_ge(s_sp, sp_wq_done)
                    eng.dma_start(out=dbg[nm], in_=src).then_inc(s_out, 16)
                    V.outv += 16

        @blk.gpsimd
        def _(eng):
            eng.wait_ge(s_dv, dv_headev[H - 1])
            eng.dma_start(out=cb_in.ap()[:, 0:H], in_=Lo_n[:, :]).then_inc(s_gp, 16)
            eng.dma_start(out=cb_in.ap()[:, H:2 * H], in_=Lo_d[:, :]).then_inc(s_gp, 16)
            V.gp += 32
            eng.wait_ge(s_gp, V.gp)
            eng.collective_compute(
                "AllGather", ALU.bypass, replica_groups=[list(range(8))],
                ins=[cb_in.ap()], outs=[cb_out.ap()],
            ).then_inc(s_cc, 1)
            V.cc += 1
            eng.wait_ge(s_cc, V.cc)
            eng.dma_start(out=Lg[:, :].rearrange("p (r n) -> p r n", r=8),
                          in_=cb_out.ap().rearrange("r p n -> p r n")).then_inc(s_gp, 16)
            V.gp += 16
        gp_carry = V.gp

        @blk.vector
        def _(eng):
            eng.wait_ge(s_gp, gp_carry)
            eng.memset(car_n[:, :], 0.0)
            eng.memset(car_d[:, :], 0.0)
            eng.nop().then_inc(s_dv, 1)
            V.dv += 1
        for j in range(8):
            dv_prev = V.dv

            @blk.vector
            def _(eng, j=j, dv_prev=dv_prev):
                eng.wait_ge(s_dv, dv_prev)
                eng.wait_ge(s_ac, V.ac)
                eng.tensor_scalar(out=tt16[:, :], in0=e_all[:, :],
                                  scalar1=cws[:, j:j + 1], scalar2=None,
                                  op0=ALU.mult).then_inc(s_dv, 1)
                V.dv += 1
            dv_t = V.dv

            @blk.scalar
            def _(eng, dv_t=dv_t):
                eng.wait_ge(s_dv, dv_t)
                eng.activation(out=co16[:, :], in_=tt16[:, :], func=AF.Exp,
                               scale=-1.0).then_inc(s_ac, 1)
                V.ac += 1
            ac_c = V.ac

            @blk.vector
            def _(eng, j=j, ac_c=ac_c):
                eng.wait_ge(s_ac, ac_c)
                eng.tensor_tensor(out=tt16[:, :], in0=co16[:, :],
                                  in1=Lg[:, j * 2 * H: j * 2 * H + H], op=ALU.mult)
                eng.drain()
                eng.tensor_tensor(out=car_n[:, :], in0=car_n[:, :], in1=tt16[:, :],
                                  op=ALU.add)
                eng.drain()
                eng.tensor_tensor(out=tt16[:, :], in0=co16[:, :],
                                  in1=Lg[:, j * 2 * H + H: (j + 1) * 2 * H], op=ALU.mult)
                eng.drain()
                eng.tensor_tensor(out=car_d[:, :], in0=car_d[:, :], in1=tt16[:, :],
                                  op=ALU.add).then_inc(s_dv, 1)
                V.dv += 1
        dv_carry = V.dv

        for h in range(H):
            @blk.vector
            def _(eng, h=h):
                eng.wait_ge(s_dv, dv_carry)
                eng.wait_ge(s_ac, V.ac)
                dbc = d_all[:, h:h + 1].broadcast_to([128, T_LOC])
                zbc = zero1[:, 0:1].broadcast_to([128, T_LOC])
                eng.tensor_tensor_scan(out=scn[:, :], data0=dbc, data1=zbc,
                                       initial=1.0, op0=ALU.mult, op1=ALU.add)
                eng.drain()
                eng.scalar_tensor_tensor(out=scd[:, :], in0=scn[:, :],
                                         scalar=car_d[:, h:h + 1],
                                         in1=denl[:, h * T_LOC:(h + 1) * T_LOC],
                                         op0=ALU.mult, op1=ALU.add)
                eng.scalar_tensor_tensor(out=scn[:, :], in0=scn[:, :],
                                         scalar=car_n[:, h:h + 1],
                                         in1=numl[:, h * T_LOC:(h + 1) * T_LOC],
                                         op0=ALU.mult, op1=ALU.add).then_inc(s_dv, 1)
                V.dv += 1
            dv_nf = V.dv

            @blk.scalar
            def _(eng, dv_nf=dv_nf):
                eng.wait_ge(s_dv, dv_nf)
                eng.activation(out=scd[:, :], in_=scd[:, :], func=AF.Ln,
                               bias=ceps6[:, 0:1])
                eng.drain()
                eng.activation(out=scd[:, :], in_=scd[:, :], func=AF.Exp,
                               scale=-1.0).then_inc(s_ac, 1)
                V.ac += 1
            ac_rd = V.ac

            @blk.vector
            def _(eng, h=h, ac_rd=ac_rd):
                eng.wait_ge(s_ac, ac_rd)
                eng.tensor_tensor(out=scn[:, :], in0=scn[:, :], in1=scd[:, :],
                                  op=ALU.mult)
                eng.drain()
                eng.tensor_tensor(out=numl[:, h * T_LOC:(h + 1) * T_LOC],
                                  in0=scn[:, :],
                                  in1=sigr[:, h * T_LOC:(h + 1) * T_LOC],
                                  op=ALU.mult).then_inc(s_dv, 1)
                V.dv += 1
        dv_rwkv = V.dv

        @blk.scalar
        def _(eng):
            eng.wait_ge(s_dv, dv_rwkv)
            for ct in range(NCT):
                eng.dma_start(out=sp_rw[ct * 128:(ct + 1) * 128, :],
                              in_=numl[:, ct * T_LOC:(ct + 1) * T_LOC]).then_inc(s_sp, 16)
                V.sp += 16
        sp_rwd = V.sp

        if "rwkvT" in dbg:
            @blk.sync
            def _(eng):
                eng.wait_ge(s_dv, dv_rwkv)
                for ct in range(NCT):
                    eng.dma_start(out=dbg["rwkvT"][ct * 128:(ct + 1) * 128, :],
                                  in_=numl[:, ct * T_LOC:(ct + 1) * T_LOC]).then_inc(s_out, 16)
                    V.outv += 16

        free("rsF1", "rsF0", "s2F1", "s2F0", "bmF1", "bmF0",
             "q8F1", "q8F0", "qoutF1", "qoutF0", "natF2", "natF1", "natF0",
             "a_h", "w_h",
             "wrt1", "wvt1", "wkt1", "wrt0", "wvt0", "wkt0", "xnT",
             "sigr", "denl", "numl", "scd", "scn")
        barrier()

        xqT = sb("xqT", [128, NCT * T_LOC], BF16)
        gwqA = sb("gwqA", [128, NCT * 512], BF16)
        uwqA = sb("uwqA", [128, NCT * 512], BF16)
        gwqB = sb("gwqB", [128, NCT * 512], BF16)
        uwqB = sb("uwqB", [128, NCT * 512], BF16)
        sg512 = sb("sg512", [128, 512], F32)
        h512 = sb("h512", [128, 512], F32)
        q8h = sb("q8h", [128, 512], FP8)
        hstage = sb("hstage", [128, NT * 512], BF16)

        @blk.sync
        def _(eng):
            for ct in range(NCT):
                eng.dma_start_transpose(
                    out=xqT[:, ct * T_LOC:(ct + 1) * T_LOC],
                    in_=sp_xq[:, ct * 128:(ct + 1) * 128]).then_inc(s_tr, 16)
                V.tr += 16
        tr_xq = V.tr

        NICH = I // 512
        tr_hch = []
        pe_chunk = []
        dv_hq = []
        sp_hch = []
        for ic in range(NICH):
            gq_b, uq_b = ((gwqA, uwqA), (gwqB, uwqB))[ic % 2]

            @blk.sync
            def _(eng, ic=ic, gq_b=gq_b, uq_b=uq_b):
                if ic >= 2:
                    eng.wait_ge(s_pe, pe_chunk[ic - 2])
                for ct in range(NCT):
                    eng.dma_start_transpose(
                        out=gq_b[:, ct * 512:(ct + 1) * 512],
                        in_=sp_gw[ic * 512:(ic + 1) * 512, ct * 128:(ct + 1) * 128],
                    ).then_inc(s_tr, 16)
                    eng.dma_start_transpose(
                        out=uq_b[:, ct * 512:(ct + 1) * 512],
                        in_=sp_uw[ic * 512:(ic + 1) * 512, ct * 128:(ct + 1) * 128],
                    ).then_inc(s_tr, 16)
                    V.tr += 32
            tr_hch.append(V.tr)

            for tt in range(NT):
                bk_g = ps[(tt % 2) * 2]
                bk_u = ps[(tt % 2) * 2 + 1]
                idx = ic * NT + tt

                @blk.tensor
                def _(eng, ic=ic, tt=tt, gq_b=gq_b, uq_b=uq_b, bk_g=bk_g,
                      bk_u=bk_u, idx=idx):
                    eng.wait_ge(s_tr, tr_hch[ic])
                    if idx >= 2:
                        eng.wait_ge(s_dv, dv_hq[idx - 2])
                    for ci in range(NCT):
                        eng.matmul(bk_g[:, :],
                                   xqT[:, ci * T_LOC + tt * 128: ci * T_LOC + tt * 128 + 128],
                                   gq_b[:, ci * 512:(ci + 1) * 512],
                                   start=(ci == 0), stop=(ci == NCT - 1))
                    mm = None
                    for ci in range(NCT):
                        mm = eng.matmul(bk_u[:, :],
                                        xqT[:, ci * T_LOC + tt * 128: ci * T_LOC + tt * 128 + 128],
                                        uq_b[:, ci * 512:(ci + 1) * 512],
                                        start=(ci == 0), stop=(ci == NCT - 1))
                    mm.then_inc(s_pe, 1)
                    V.pe += 1
                pe_gu = V.pe

                @blk.scalar
                def _(eng, pe_gu=pe_gu, bk_g=bk_g):
                    eng.wait_ge(s_pe, pe_gu)
                    eng.wait_ge(s_dv, V.dv)
                    eng.activation(out=sg512[:, :], in_=bk_g[:, :],
                                   func=AF.Silu).then_inc(s_ac, 1)
                    V.ac += 1
                ac_sg = V.ac

                @blk.vector
                def _(eng, tt=tt, ac_sg=ac_sg, bk_u=bk_u, ic=ic):
                    eng.wait_ge(s_ac, ac_sg)
                    if sp_hch and tt == 0:
                        eng.wait_ge(s_sp, sp_hch[-1])
                    eng.tensor_tensor(out=h512[:, :], in0=sg512[:, :], in1=bk_u[:, :],
                                      op=ALU.mult)
                    eng.drain()
                    eng.tensor_reduce(out=bm64[:, 0:16],
                                      in_=h512[:, :].rearrange("p (b k) -> p b k", k=32),
                                      axis=AX.X, op=ALU.max, apply_absolute_value=True)
                    eng.drain()
                    eng.tensor_scalar(out=s2x[:, 0:16], in0=bm64[:, 0:16], scalar1=R448_2,
                                      scalar2=2e-30, op0=ALU.mult, op1=ALU.max)
                    eng.drain()
                    eng.reciprocal(out=rs2[:, 0:16], in_=s2x[:, 0:16])
                    eng.drain()
                    eng.tensor_tensor(out=q8h[:, :].rearrange("p (b k) -> p b k", k=32),
                                      in0=h512[:, :].rearrange("p (b k) -> p b k", k=32),
                                      in1=rs2[:, 0:16].rearrange("p (b o) -> p b o", o=1
                                          ).broadcast_to([128, 16, 32]),
                                      op=ALU.mult)
                    eng.drain()
                    eng.tensor_tensor(out=hstage[:, tt * 512:(tt + 1) * 512
                                      ].rearrange("p (b k) -> p b k", k=32),
                                      in0=q8h[:, :].rearrange("p (b k) -> p b k", k=32),
                                      in1=s2x[:, 0:16].rearrange("p (b o) -> p b o", o=1
                                          ).broadcast_to([128, 16, 32]),
                                      op=ALU.mult).then_inc(s_dv, 1)
                    V.dv += 1
                dv_hq.append(V.dv)
            pe_chunk.append(V.pe)

            @blk.scalar
            def _(eng, ic=ic):
                eng.wait_ge(s_dv, dv_hq[-1])
                eng.dma_start(
                    out=sp_hq.rearrange("(t p) i -> p t i", p=128)[:, :, ic * 512:(ic + 1) * 512],
                    in_=hstage[:, :].rearrange("p (t n) -> p t n", t=NT),
                ).then_inc(s_sp, 16)
                V.sp += 16
            sp_hch.append(V.sp)
        sp_hqd = V.sp

        if "hq" in dbg:
            @blk.sync
            def _(eng):
                eng.wait_ge(s_sp, sp_hqd)
                eng.dma_start(out=dbg["hq"], in_=sp_hq).then_inc(s_out, 16)
                V.outv += 16

        free("hstage", "q8h", "h512", "sg512", "uwqB", "gwqB", "uwqA", "gwqA",
             "xqT")
        barrier()

        acc = sb("acc", [128, NT * C], F32)
        hqgA = sb("hqgA", [128, 8 * T_LOC], BF16)
        hqgB = sb("hqgB", [128, 8 * T_LOC], BF16)
        dwqA = sb("dwqA", [128, 8 * C], BF16)
        dwqB = sb("dwqB", [128, 8 * C], BF16)

        @blk.vector
        def _(eng):
            eng.memset(acc[:, :], 0.0)
            eng.nop().then_inc(s_dv, 1)
            V.dv += 1
        dv_accz = V.dv

        NG8 = 8
        tr_g8 = []
        pe_g8 = []
        dv_unit = []
        pe_unit = []
        for g8 in range(NG8):
            hq_b = (hqgA, hqgB)[g8 % 2]
            dq_b = (dwqA, dwqB)[g8 % 2]

            @blk.sync
            def _(eng, g8=g8, hq_b=hq_b, dq_b=dq_b):
                if g8 >= 2:
                    eng.wait_ge(s_pe, pe_g8[g8 - 2])
                for k in range(8):
                    it = g8 * 8 + k
                    eng.dma_start_transpose(
                        out=hq_b[:, k * T_LOC:(k + 1) * T_LOC],
                        in_=sp_hq[:, it * 128:(it + 1) * 128]).then_inc(s_tr, 16)
                    eng.dma_start_transpose(
                        out=dq_b[:, k * C:(k + 1) * C],
                        in_=sp_dw[:, it * 128:(it + 1) * 128]).then_inc(s_tr, 16)
                    V.tr += 32
            tr_g8.append(V.tr)

            for u in range(8):
                ncn = u // 2
                qd = u % 2
                half = (u % 2)
                gu = g8 * 8 + u

                @blk.tensor
                def _(eng, g8=g8, u=u, ncn=ncn, qd=qd, half=half, gu=gu,
                      hq_b=hq_b, dq_b=dq_b):
                    eng.wait_ge(s_tr, tr_g8[g8])
                    if gu == 0:
                        eng.wait_ge(s_dv, dv_accz)
                    if gu >= 2:
                        eng.wait_ge(s_dv, dv_unit[gu - 2])
                    mm = None
                    for tti in range(4):
                        tt = qd * 4 + tti
                        for k in range(8):
                            mm = eng.matmul(ps[half * 4 + tti][:, :],
                                            hq_b[:, k * T_LOC + tt * 128: k * T_LOC + tt * 128 + 128],
                                            dq_b[:, k * C + ncn * 512: k * C + ncn * 512 + 512],
                                            start=(k == 0), stop=(k == 7))
                    mm.then_inc(s_pe, 1)
                    V.pe += 1
                pe_unit.append(V.pe)

                @blk.vector
                def _(eng, ncn=ncn, qd=qd, half=half, pe_u=V.pe):
                    eng.wait_ge(s_pe, pe_u)
                    for tti in range(4):
                        tt = qd * 4 + tti
                        sl = slice(tt * C + ncn * 512, tt * C + ncn * 512 + 512)
                        eng.tensor_tensor(out=acc[:, sl], in0=acc[:, sl],
                                          in1=ps[half * 4 + tti][:, :], op=ALU.add)
                    eng.nop().then_inc(s_dv, 1)
                    V.dv += 1
                dv_unit.append(V.dv)
            pe_g8.append(V.pe)
        dv_down = V.dv

        free("dwqB", "dwqA", "hqgB", "hqgA")
        barrier()

        rwT = sb("rwT", [128, NCT * T_LOC], BF16)
        wotc = sb("wotc", [128, NCT * 512], BF16)
        xbufJ = sb("xbufJ", [128, NT * C], F32)
        gsil = sb("gsil", [128, C], F32)
        otile = sb("otile", [128, 2 * 512], F32)

        @blk.scalar
        def _(eng):
            eng.wait_ge(s_sp, sp_rwd)
            eng.dma_start(out=rwT[:, :].rearrange("p (c n) -> p c n", c=NCT),
                          in_=sp_rw.rearrange("(c p) n -> p c n", p=128)).then_inc(s_ld, 16)
            eng.dma_start(out=xbufJ[:, :].rearrange("p (t c) -> p t c", t=NT),
                          in_=x_tm.rearrange("(t p) c -> p t c", p=128)).then_inc(s_ld, 16)
            V.ld += 32
        ld_rw = V.ld

        gp_wot = []
        pe_s = []
        dv_fin = []
        for ncn in range(4):
            @blk.gpsimd
            def _(eng, ncn=ncn):
                if ncn >= 1:
                    eng.wait_ge(s_pe, pe_s[-1])
                eng.dma_start(out=wotc[:, :].rearrange("p (c n) -> p c n", c=NCT),
                              in_=WoT.rearrange("(c p) n -> p c n", p=128)[:, :, ncn * 512:(ncn + 1) * 512],
                              ).then_inc(s_gp, 16)
                V.gp += 16
            gp_wot.append(V.gp)

            for tt in range(NT):
                bank = ps[tt % 4]
                idx = ncn * NT + tt

                @blk.tensor
                def _(eng, ncn=ncn, tt=tt, bank=bank, idx=idx):
                    eng.wait_ge(s_gp, gp_wot[ncn])
                    eng.wait_ge(s_ld, ld_rw)
                    if idx >= 4:
                        eng.wait_ge(s_dv, dv_fin[idx - 4])
                    mm = None
                    for ci in range(NCT):
                        mm = eng.matmul(bank[:, :],
                                        rwT[:, ci * T_LOC + tt * 128: ci * T_LOC + tt * 128 + 128],
                                        wotc[:, ci * 512:(ci + 1) * 512],
                                        start=(ci == 0), stop=(ci == NCT - 1))
                    mm.then_inc(s_pe, 1)
                    V.pe += 1
                pe_s.append(V.pe)
                pe_this = V.pe

                @blk.scalar
                def _(eng, ncn=ncn, tt=tt, pe_this=pe_this):
                    eng.wait_ge(s_pe, pe_this)
                    eng.wait_ge(s_dv, V.dv)
                    eng.activation(out=gsil[:, ncn * 512:(ncn + 1) * 512],
                                   in_=acc[:, tt * C + ncn * 512: tt * C + ncn * 512 + 512],
                                   func=AF.Silu).then_inc(s_ac, 1)
                    V.ac += 1
                ac_g = V.ac

                slot = slice((idx % 2) * 512, (idx % 2) * 512 + 512)
                out_done = V.outv

                @blk.vector
                def _(eng, ncn=ncn, tt=tt, ac_g=ac_g, bank=bank, slot=slot,
                      out_done=out_done, idx=idx):
                    eng.wait_ge(s_ac, ac_g)
                    if idx >= 2:
                        eng.wait_ge(s_out, out_done - 16)
                    gl = slice(ncn * 512, (ncn + 1) * 512)
                    eng.tensor_tensor(out=otile[:, slot], in0=bank[:, :],
                                      in1=gsil[:, gl], op=ALU.mult)
                    eng.drain()
                    eng.tensor_tensor(out=otile[:, slot], in0=otile[:, slot],
                                      in1=xbufJ[:, tt * C + ncn * 512: tt * C + ncn * 512 + 512],
                                      op=ALU.add).then_inc(s_dv, 1)
                    V.dv += 1
                dv_fin.append(V.dv)

                @blk.scalar
                def _(eng, ncn=ncn, tt=tt, slot=slot):
                    eng.wait_ge(s_dv, dv_fin[-1])
                    eng.dma_start(out=out[tt * 128:(tt + 1) * 128, ncn * 512:(ncn + 1) * 512],
                                  in_=otile[:, slot]).then_inc(s_out, 16)
                    V.outv += 16

        @blk.scalar
        def _(eng):
            eng.wait_ge(s_out, V.outv)

    for n in reversed(list(live)):
        live[n].__exit__(None, None, None)
    for cm in reversed(ctx):
        cm.__exit__(None, None, None)
    return nc


def make_inmaps(inputs):
    x = np.asarray(inputs["L0_attn_x"], dtype=np.float32).reshape(8192, 2048)
    WkT = np.ascontiguousarray(np.asarray(inputs["L0_attn_Wk"]).T)
    WvT = np.ascontiguousarray(np.asarray(inputs["L0_attn_Wv"]).T)
    WrT = np.ascontiguousarray(np.asarray(inputs["L0_attn_Wr"]).T)
    WoT = np.ascontiguousarray(np.asarray(inputs["L0_attn_Wo"]).T)
    gw = np.ascontiguousarray(np.asarray(inputs["L0_ffn_gate_w"], dtype=np.float32))
    uw = np.ascontiguousarray(np.asarray(inputs["L0_ffn_up_w"], dtype=np.float32))
    dw = np.ascontiguousarray(np.asarray(inputs["L0_ffn_down_w"], dtype=np.float32))
    tdT = np.ascontiguousarray(np.asarray(inputs["L0_attn_time_decay"]).T.astype(np.float32))
    tfT = np.ascontiguousarray(np.asarray(inputs["L0_attn_time_first"]).T.astype(np.float32))
    maps = []
    for i in range(8):
        cw_row = np.zeros(8, np.float32)
        for j in range(8):
            cw_row[j] = 1024.0 * (i - 1 - j) if j < i else 1e9
        maps.append({
            "x_tm": np.ascontiguousarray(x[i * 1024:(i + 1) * 1024]),
            "WkT": WkT, "WvT": WvT, "WrT": WrT, "WoT": WoT,
            "gw": gw, "uw": uw, "dw": dw,
            "tdT": tdT, "tfT": tfT,
            "cwi": np.tile(cw_row, (128, 1)),
            "c0f": np.full((128, 1), 1.0 if i == 0 else 0.0, np.float32),
        })
    return maps


def kernel(**inputs):
    nc = build_nc()
    maps = make_inmaps(inputs)
    res = run_bass_kernel_spmd(nc, maps, core_ids=list(range(8)))
    o = np.concatenate([np.asarray(r["out"]) for r in res.results], axis=0)
    return o.reshape(1, 8192, 2048).astype(np.float32)
